# version 03 (final)
# baseline (speedup 1.0000x reference)
import contextlib

import numpy as np
import ml_dtypes

import concourse.bass as bass
import concourse.mybir as mybir
from concourse.bass_utils import run_bass_kernel_spmd

F32 = mybir.dt.float32
BF16 = mybir.dt.bfloat16
AF = mybir.ActivationFunctionType
ALU = mybir.AluOpType
AX = mybir.AxisListType
BF = ml_dtypes.bfloat16

N_CORES = 8
D_MODEL = 2048
BATCH = 4
TOK = 2048
NCH = D_MODEL // 128


_SEMS = ("pe", "dv", "ac", "acd", "gp", "gpd", "sy")


class Buf:
    __slots__ = ("name", "writer", "readers")

    def __init__(self, name):
        self.name = name
        self.writer = None
        self.readers = []


class Op:
    __slots__ = ("eng", "fn", "deps", "idx", "dma", "needs_inc", "val", "semid", "snapshot")


class Prog:
    SERIALIZE = False
    BARRIER_AFTER = frozenset()

    def __init__(self, nc):
        self.nc = nc
        self.ops = []

    def buf(self, name):
        return Buf(name)

    def barrier(self):
        o = Op()
        o.eng, o.fn, o.dma = "BARRIER", None, False
        o.needs_inc = False
        o.val = None
        o.idx = len(self.ops)
        o.deps = []
        self.ops.append(o)

    def op(self, eng, fn, reads=(), writes=(), dma=False):
        if self.ops and self.ops[-1].eng != "BARRIER":
            prev = self.ops[-1]
            if Prog.SERIALIZE or (prev.eng in Prog.BARRIER_AFTER and not prev.dma):
                self.barrier()
        o = Op()
        o.eng, o.fn, o.dma = eng, fn, dma
        o.needs_inc = False
        o.val = None
        o.idx = len(self.ops)
        self.ops.append(o)
        deps = set()
        for b in reads:
            if b.writer is not None:
                deps.add(b.writer)
        for b in writes:
            if b.writer is not None:
                deps.add(b.writer)
            deps.update(b.readers)
        deps.discard(o.idx)
        o.deps = [d for d in deps
                  if not (eng == "pe" and self.ops[d].eng == "pe")]
        for b in reads:
            b.readers.append(o.idx)
        for b in writes:
            b.writer = o.idx
            b.readers = []
        return o

    TRUNCATE = None

    def emit(self, block):
        ops = self.ops
        if Prog.TRUNCATE is not None:
            ops = ops[:Prog.TRUNCATE]
            keep = set(range(len(ops)))
            for o in ops:
                o.deps = [d for d in o.deps if d in keep]
        has_barrier = any(o.eng == "BARRIER" for o in ops)
        for o in ops:
            if o.eng == "BARRIER":
                continue
            if o.dma or has_barrier:
                o.needs_inc = True
            for d in o.deps:
                ops[d].needs_inc = True

        NROT = 4
        dma_queues = ("syd", "acd", "gpd")
        sems = {}
        for k in ("pe", "dv", "ac", "gp"):
            sems[k] = self.nc.alloc_semaphore(f"sem_{k}")
        for k in dma_queues:
            for r in range(NROT):
                sems[(k, r)] = self.nc.alloc_semaphore(f"sem_{k}{r}")

        def semkey(o):
            if o.dma:
                qname = {"ac": "acd", "gp": "gpd", "sy": "syd"}[o.eng]
                return qname
            return o.eng

        count = {}
        dma_idx = {k: 0 for k in dma_queues}
        for o in ops:
            if o.eng == "BARRIER":
                o.snapshot = dict(count)
                for k, j in dma_idx.items():
                    for r in range(NROT):
                        nfull = (j - 1 - r) // NROT + 1 if j > r else 0
                        o.snapshot[(k, r)] = 16 * nfull
                continue
            if not o.needs_inc:
                continue
            k = semkey(o)
            if o.dma:
                j = dma_idx[k]
                dma_idx[k] = j + 1
                o.semid = (k, j % NROT)
                o.val = 16 * (j // NROT + 1)
            else:
                count[k] = count.get(k, 0) + 1
                o.semid = k
                o.val = count[k]
        final = {}
        for o in ops:
            if o.needs_inc:
                final[o.semid] = max(final.get(o.semid, 0), o.val)

        import bisect as _bisect
        inc_idx_by_sem = {}
        for o in ops:
            if o.needs_inc:
                inc_idx_by_sem.setdefault(o.semid, []).append(o.idx)
        def bump_pe(d, consumer_idx):
            lst = inc_idx_by_sem.get(ops[d].semid, [])
            pos = _bisect.bisect_right(lst, d)
            if pos < len(lst) and lst[pos] < consumer_idx:
                return lst[pos]
            return d

        def emit_for(eng, q):
            waited = {}
            for o in ops:
                if o.eng == "BARRIER":
                    for k, v in o.snapshot.items():
                        if v > 0 and waited.get(k, 0) < v:
                            q.wait_ge(sems[k], v)
                            waited[k] = v
                    continue
                if o.eng != eng:
                    continue
                if o.dma and o.needs_inc and o.val > 16:
                    k = o.semid
                    if waited.get(k, 0) < o.val - 16:
                        q.wait_ge(sems[k], o.val - 16)
                        waited[k] = o.val - 16
                for d in sorted(o.deps,
                                key=lambda i: (str(ops[i].semid), ops[i].val)):
                    pr = ops[d]
                    if pr.eng != eng:
                        pr = ops[bump_pe(d, o.idx)]
                    k = pr.semid
                    if waited.get(k, 0) < pr.val:
                        q.wait_ge(sems[k], pr.val)
                        waited[k] = pr.val
                instr = o.fn(q)
                if o.needs_inc:
                    instr.then_inc(sems[o.semid], 16 if o.dma else 1)
            if eng == "sy":
                for k, v in final.items():
                    if waited.get(k, 0) < v:
                        q.wait_ge(sems[k], v)

        @block.tensor
        def _(q):
            emit_for("pe", q)

        @block.vector
        def _(q):
            emit_for("dv", q)

        @block.scalar
        def _(q):
            emit_for("ac", q)

        @block.gpsimd
        def _(q):
            emit_for("gp", q)

        @block.sync
        def _(q):
            emit_for("sy", q)


def alloc(ctx, nc, p, name, shape, dtype, psum=False):
    if psum:
        t = ctx.enter_context(nc.psum_tensor(name, shape, dtype))
    else:
        t = ctx.enter_context(nc.sbuf_tensor(name, shape, dtype))
    return t, p.buf(name)



class LNScratch:

    def __init__(self, ctx, nc, p, T, pools, tag):
        self.T = T
        self.ps_s, self.ps_s_b = pools["row"]
        self.ps_mb, self.ps_mb_b = pools["bc_m"]
        self.ps_rb, self.ps_rb_b = pools["bc_r"]
        self.xb_t, self.xb_b = alloc(ctx, nc, p, f"{tag}_xb", [128, 2 * 512], BF16)
        self.xb_b2 = p.buf(f"{tag}_xb2")
        self.sq_t, self.sq_b = alloc(ctx, nc, p, f"{tag}_sq", [128, 2 * 512], BF16)
        self.sq_b2 = p.buf(f"{tag}_sq2")
        self.mrow_t, self.mrow_b = alloc(ctx, nc, p, f"{tag}_mr", [1, 512], F32)
        self.vrow_t, self.vrow_b = alloc(ctx, nc, p, f"{tag}_vr", [1, 512], F32)
        self.trow_t, self.trow_b = alloc(ctx, nc, p, f"{tag}_tr", [1, 512], F32)
        self.tmp_t, self.tmp_b = alloc(ctx, nc, p, f"{tag}_tp", [128, 512], F32)

    def run(self, p, c, x_t, x_b, xn_t, xn_b, T, eps=1e-5, aug_t=None, aug_b=None):
        s = self
        xbb = [s.xb_b, s.xb_b2]
        sqb = [s.sq_b, s.sq_b2]
        for tt in range(T // 512):
            for ch in range(NCH):
                o = ch * T + tt * 512
                sl = slice((ch % 2) * 512, (ch % 2) * 512 + 512)
                p.op("dv", lambda q, o=o, sl=sl: q.tensor_copy(
                    s.xb_t[:, sl], x_t[:, o:o + 512]),
                    reads=[x_b], writes=[xbb[ch % 2]])
                p.op("pe", lambda q, ch=ch, sl=sl: q.matmul(
                    s.ps_s[:, :], c["ones"][:, 0:1], s.xb_t[:, sl],
                    start=(ch == 0), stop=(ch == NCH - 1)),
                    reads=[xbb[ch % 2], c["ones_b"]], writes=[s.ps_s_b])
                p.op("ac", lambda q, sl=sl: q.activation(
                    s.sq_t[:, sl], s.xb_t[:, sl], AF.Square),
                    reads=[xbb[ch % 2]], writes=[sqb[ch % 2]])
                p.op("pe", lambda q, ch=ch, sl=sl: q.matmul(
                    s.ps_mb[0:1, :], c["ones"][:, 0:1], s.sq_t[:, sl],
                    start=(ch == 0), stop=(ch == NCH - 1)),
                    reads=[sqb[ch % 2], c["ones_b"]], writes=[s.ps_mb_b])
            p.op("dv", lambda q: q.tensor_scalar_mul(
                s.mrow_t[:, :], s.ps_s[:, :], 1.0 / D_MODEL),
                reads=[s.ps_s_b], writes=[s.mrow_b])
            p.op("dv", lambda q: q.tensor_scalar_mul(
                s.vrow_t[:, :], s.ps_mb[0:1, :], 1.0 / D_MODEL),
                reads=[s.ps_mb_b], writes=[s.vrow_b])
            p.op("dv", lambda q: q.tensor_mul(
                s.trow_t[:, :], s.mrow_t[:, :], s.mrow_t[:, :]),
                reads=[s.mrow_b], writes=[s.trow_b])
            p.op("dv", lambda q: q.tensor_sub(
                s.vrow_t[:, :], s.vrow_t[:, :], s.trow_t[:, :]),
                reads=[s.vrow_b, s.trow_b], writes=[s.vrow_b])
            p.op("dv", lambda q: q.tensor_scalar_add(
                s.vrow_t[:, :], s.vrow_t[:, :], float(eps)),
                reads=[s.vrow_b], writes=[s.vrow_b])
            p.op("ac", lambda q: q.activation(
                s.vrow_t[:, :], s.vrow_t[:, :], AF.Ln),
                reads=[s.vrow_b], writes=[s.vrow_b])
            p.op("ac", lambda q: q.activation(
                s.vrow_t[:, :], s.vrow_t[:, :], AF.Exp, scale=-0.5),
                reads=[s.vrow_b], writes=[s.vrow_b])
            p.op("pe", lambda q: q.matmul(
                s.ps_mb[:, :], c["onef"][0:1, :], s.mrow_t[:, :],
                start=True, stop=True),
                reads=[s.mrow_b, c["onef_b"]], writes=[s.ps_mb_b])
            p.op("pe", lambda q: q.matmul(
                s.ps_rb[:, :], c["onef"][0:1, :], s.vrow_t[:, :],
                start=True, stop=True),
                reads=[s.vrow_b, c["onef_b"]], writes=[s.ps_rb_b])
            for ch in range(NCH):
                o = ch * T + tt * 512
                p.op("dv", lambda q, o=o: q.tensor_sub(
                    s.tmp_t[:, :], x_t[:, o:o + 512], s.ps_mb[:, :]),
                    reads=[x_b, s.ps_mb_b], writes=[s.tmp_b])
                p.op("dv", lambda q, o=o: q.tensor_mul(
                    xn_t[:, o:o + 512], s.tmp_t[:, :], s.ps_rb[:, :]),
                    reads=[s.tmp_b, s.ps_rb_b], writes=[xn_b])
                if aug_t is not None:
                    p.op("dv", lambda q, o=o: q.tensor_mul(
                        s.tmp_t[:, :], s.tmp_t[:, :], s.ps_rb[:, :]),
                        reads=[s.tmp_b, s.ps_rb_b], writes=[s.tmp_b])
                    p.op("dv", lambda q, o=o: q.tensor_add(
                        aug_t[:, o:o + 512], x_t[:, o:o + 512], s.tmp_t[:, :]),
                        reads=[x_b, s.tmp_b], writes=[aug_b])


def make_common(ctx, nc, p):
    onescol = nc.inline_tensor(np.ones((128, 1), BF), "c_onescol").ap()
    onefrow = nc.inline_tensor(np.ones((1, 128), np.float32), "c_onefrow").ap()
    idf = nc.inline_tensor(np.eye(128, dtype=np.float32), "c_idf").ap()
    ones_t, ones_b = alloc(ctx, nc, p, "ones", [128, 1], BF16)
    onef_t, onef_b = alloc(ctx, nc, p, "onef", [1, 128], F32)
    idf_t, idf_b = alloc(ctx, nc, p, "idf", [128, 128], F32)
    p.op("sy", lambda q: q.dma_start(out=ones_t[:, :], in_=onescol[:, :]),
         writes=[ones_b], dma=True)
    p.op("sy", lambda q: q.dma_start(out=onef_t[:, :], in_=onefrow[:, :]),
         writes=[onef_b], dma=True)
    p.op("sy", lambda q: q.dma_start(out=idf_t[:, :], in_=idf[:, :]),
         writes=[idf_b], dma=True)
    return {"ones": ones_t, "ones_b": ones_b,
            "onef": onef_t, "onef_b": onef_b,
            "idf": idf_t, "idf_b": idf_b}



def build_ffn0():
    nc = bass.Bass("TRN2", target_bir_lowering=False)
    QT = 512
    NP = TOK // QT

    x_d = nc.dram_tensor("x", [D_MODEL, TOK], F32, kind="ExternalInput").ap()
    wsh_up = nc.dram_tensor("wsh_up", [2, D_MODEL, 1024], BF16, kind="ExternalInput").ap()
    wsh_dn = nc.dram_tensor("wsh_dn", [2, 1024, D_MODEL], BF16, kind="ExternalInput").ap()
    wrt_up = nc.dram_tensor("wrt_up", [16, D_MODEL, 1024], BF16, kind="ExternalInput").ap()
    wrt_dn = nc.dram_tensor("wrt_dn", [16, 1024, D_MODEL], BF16, kind="ExternalInput").ap()
    wrout = nc.dram_tensor("wrout", [D_MODEL, 16], BF16, kind="ExternalInput").ap()
    x1_d = nc.dram_tensor("x1", [D_MODEL, TOK], F32, kind="ExternalOutput").ap()
    gate_dram = nc.dram_tensor("gate_scr", [16, TOK], F32, kind="Internal").ap()

    p = Prog(nc)
    ctx = contextlib.ExitStack()
    with ctx:
        c = make_common(ctx, nc, p)
        pools = {}
        pools["row"] = alloc(ctx, nc, p, "ps_row", [1, 512], F32, psum=True)
        pools["bc_m"] = alloc(ctx, nc, p, "ps_bcm", [128, 512], F32, psum=True)
        pools["bc_r"] = alloc(ctx, nc, p, "ps_bcr", [128, 512], F32, psum=True)
        ps_sm, ps_sm_b = alloc(ctx, nc, p, "ps_sm", [128, 128], F32, psum=True)
        up_ps, up_ps_b = alloc(ctx, nc, p, "ps_up", [128, 512], F32, psum=True)
        dn_ps, dn_ps_b = alloc(ctx, nc, p, "ps_dn", [128, 512], F32, psum=True)
        up2_ps, up2_ps_b = alloc(ctx, nc, p, "ps_up2", [128, 512], F32, psum=True)
        dn2_ps, dn2_ps_b = alloc(ctx, nc, p, "ps_dn2", [128, 512], F32, psum=True)
        upp = [(up_ps, up_ps_b), (up2_ps, up2_ps_b)]
        dnp = [(dn_ps, dn_ps_b), (dn2_ps, dn2_ps_b)]

        ln = LNScratch(ctx, nc, p, QT, {"row": pools["row"],
                                        "bc_m": pools["bc_m"],
                                        "bc_r": pools["bc_r"]}, "ln")

        x_t, x_b = alloc(ctx, nc, p, "xs", [128, NCH * QT], F32)
        xn_t, xn_b = alloc(ctx, nc, p, "xn", [128, NCH * QT], BF16)
        mid_t, mid_b = alloc(ctx, nc, p, "mid", [128, 8 * QT], BF16)
        gelu_t, gelu_b = alloc(ctx, nc, p, "gelu", [128, 512], F32)
        wup_t, wup_b = alloc(ctx, nc, p, "wup", [128, 16 * 1024], BF16)
        wdn_t, wdn_b = alloc(ctx, nc, p, "wdn", [128, 8 * 2048], BF16)
        wr_t, wr_b = alloc(ctx, nc, p, "wr", [128, 16 * 16], BF16)
        gmax_t, gmax_b = alloc(ctx, nc, p, "gmax", [128, 1], F32)
        gexp_t, gexp_b = alloc(ctx, nc, p, "gexp", [128, 16], F32)
        gsum_t, gsum_b = alloc(ctx, nc, p, "gsum", [128, 1], F32)
        grec_t, grec_b = alloc(ctx, nc, p, "grec", [128, 1], F32)
        gm1m_t, gm1m_b = alloc(ctx, nc, p, "gm1m", [128, 16], F32)
        gneg_t, gneg_b = alloc(ctx, nc, p, "gneg", [128, 16], F32)
        gm2_t, gm2_b = alloc(ctx, nc, p, "gm2", [128, 1], F32)
        gmask_t, gmask_b = alloc(ctx, nc, p, "gmask", [128, 16], F32)
        gtile_t, gtile_b = alloc(ctx, nc, p, "gtile", [128, 16], F32)
        gfm_t, gfm_b = alloc(ctx, nc, p, "gfm", [16, QT], F32)
        grow_t, grow_b = alloc(ctx, nc, p, "grow", [1, QT], F32)
        gbc_t, gbc_b = alloc(ctx, nc, p, "gbc", [128, QT], F32)
        gate_d_b = p.buf("gate_dram")

        p.op("sy", lambda q: q.dma_start(
            out=wr_t[:, :].rearrange("p (a m) -> p a m", a=16),
            in_=wrout.rearrange("(a p) m -> p a m", p=128)),
            writes=[wr_b], dma=True)

        experts = [("sh", e) for e in range(2)] + [("rt", e) for e in range(16)]

        for pi in range(NP):
            t0 = pi * QT
            for ch in range(NCH):
                p.op("sy", lambda q, ch=ch, t0=t0: q.dma_start(
                    out=x_t[:, ch * QT:(ch + 1) * QT],
                    in_=x_d[ch * 128:(ch + 1) * 128, t0:t0 + QT]),
                    writes=[x_b], dma=True)
            ln.run(p, c, x_t, x_b, xn_t, xn_b, QT)

            for tt in range(QT // 128):
                for ch in range(NCH):
                    p.op("pe", lambda q, ch=ch, tt=tt: q.matmul(
                        ps_sm[:, 0:16],
                        xn_t[:, ch * QT + tt * 128:ch * QT + tt * 128 + 128],
                        wr_t[:, ch * 16:ch * 16 + 16],
                        start=(ch == 0), stop=(ch == NCH - 1)),
                        reads=[xn_b, wr_b], writes=[ps_sm_b])
                p.op("dv", lambda q: q.reduce_max(
                    gmax_t[:, :], ps_sm[:, 0:16], AX.X),
                    reads=[ps_sm_b], writes=[gmax_b])
                p.op("dv", lambda q: q.tensor_scalar_mul(
                    gmax_t[:, :], gmax_t[:, :], -1.0),
                    reads=[gmax_b], writes=[gmax_b])
                p.op("ac", lambda q: q.activation(
                    gexp_t[:, :], ps_sm[:, 0:16], AF.Exp, bias=gmax_t[:, 0:1]),
                    reads=[ps_sm_b, gmax_b], writes=[gexp_b])
                p.op("dv", lambda q: q.reduce_sum(
                    gsum_t[:, :], gexp_t[:, :], AX.X),
                    reads=[gexp_b], writes=[gsum_b])
                p.op("dv", lambda q: q.reciprocal(grec_t[:, :], gsum_t[:, :]),
                     reads=[gsum_b], writes=[grec_b])
                p.op("dv", lambda q: q.reduce_max(gm2_t[:, :], gexp_t[:, :], AX.X),
                     reads=[gexp_b], writes=[gm2_b])
                p.op("dv", lambda q: q.tensor_scalar(
                    gm1m_t[:, :], gexp_t[:, :], gm2_t[:, 0:1], None, ALU.is_ge),
                    reads=[gexp_b, gm2_b], writes=[gm1m_b])
                p.op("dv", lambda q: q.scalar_tensor_tensor(
                    gneg_t[:, :], gm1m_t[:, :], -1e30, gexp_t[:, :],
                    ALU.mult, ALU.add),
                    reads=[gm1m_b, gexp_b], writes=[gneg_b])
                p.op("dv", lambda q: q.reduce_max(gm2_t[:, :], gneg_t[:, :], AX.X),
                     reads=[gneg_b], writes=[gm2_b])
                p.op("dv", lambda q: q.tensor_scalar(
                    gmask_t[:, :], gexp_t[:, :], gm2_t[:, 0:1], None, ALU.is_ge),
                    reads=[gexp_b, gm2_b], writes=[gmask_b])
                p.op("dv", lambda q: q.tensor_mul(
                    gtile_t[:, :], gexp_t[:, :], gmask_t[:, :]),
                    reads=[gexp_b, gmask_b], writes=[gtile_b])
                p.op("dv", lambda q: q.tensor_scalar(
                    gtile_t[:, :], gtile_t[:, :], grec_t[:, 0:1], None, ALU.mult),
                    reads=[gtile_b, grec_b], writes=[gtile_b])
                p.op("pe", lambda q: q.transpose(
                    ps_sm[0:16, 0:128], gtile_t[:, :], c["idf"][:, :]),
                    reads=[gtile_b, c["idf_b"]], writes=[ps_sm_b])
                p.op("dv", lambda q, tt=tt: q.tensor_copy(
                    gfm_t[:, tt * 128:tt * 128 + 128], ps_sm[0:16, 0:128]),
                    reads=[ps_sm_b], writes=[gfm_b])
            p.op("sy", lambda q, t0=t0: q.dma_start(
                out=gate_dram[:, t0:t0 + QT], in_=gfm_t[:, :]),
                reads=[gfm_b], writes=[gate_d_b], dma=True)

            for ei, (kind, e) in enumerate(experts):
                wu_src = wsh_up if kind == "sh" else wrt_up
                wd_src = wsh_dn if kind == "sh" else wrt_dn
                p.op("sy", lambda q, e=e, wu_src=wu_src: q.dma_start(
                    out=wup_t[:, :].rearrange("p (a m) -> p a m", a=16),
                    in_=wu_src[e].rearrange("(a p) m -> p a m", p=128)),
                    writes=[wup_b], dma=True)
                p.op("ac", lambda q, e=e, wd_src=wd_src: q.dma_start(
                    out=wdn_t[:, :].rearrange("p (a m) -> p a m", a=8),
                    in_=wd_src[e].rearrange("(a p) m -> p a m", p=128)),
                    writes=[wdn_b], dma=True)
                if kind == "rt":
                    p.op("sy", lambda q, e=e, t0=t0: q.dma_start(
                        out=grow_t[:, :], in_=gate_dram[e:e + 1, t0:t0 + QT]),
                        reads=[gate_d_b], writes=[grow_b], dma=True)
                    p.op("pe", lambda q: q.matmul(
                        pools["bc_m"][0][:, :], c["onef"][0:1, :],
                        grow_t[:, 0:512], start=True, stop=True),
                        reads=[grow_b, c["onef_b"]], writes=[pools["bc_m"][1]])
                    p.op("dv", lambda q: q.tensor_copy(
                        gbc_t[:, 0:512], pools["bc_m"][0][:, :]),
                        reads=[pools["bc_m"][1]], writes=[gbc_b])
                for ic in range(8):
                    pp, pp_b = upp[ic % 2]
                    for dc in range(NCH):
                        p.op("pe", lambda q, ic=ic, dc=dc, pp=pp: q.matmul(
                            pp[:, :],
                            wup_t[:, dc * 1024 + ic * 128:dc * 1024 + ic * 128 + 128],
                            xn_t[:, dc * QT:dc * QT + 512],
                            start=(dc == 0), stop=(dc == NCH - 1)),
                            reads=[wup_b, xn_b], writes=[pp_b])
                    if kind == "sh":
                        p.op("ac", lambda q, ic=ic, pp=pp: q.activation(
                            mid_t[:, ic * QT:ic * QT + 512], pp[:, :], AF.Gelu),
                            reads=[pp_b], writes=[mid_b])
                    else:
                        p.op("ac", lambda q, ic=ic, pp=pp: q.activation(
                            gelu_t[:, :], pp[:, :], AF.Gelu),
                            reads=[pp_b], writes=[gelu_b])
                        p.op("gp", lambda q, ic=ic: q.tensor_mul(
                            mid_t[:, ic * QT:ic * QT + 512], gelu_t[:, :],
                            gbc_t[:, 0:512]),
                            reads=[gelu_b, gbc_b], writes=[mid_b])
                for dc in range(NCH):
                    pp, pp_b = dnp[dc % 2]
                    for ic in range(8):
                        p.op("pe", lambda q, dc=dc, ic=ic, pp=pp: q.matmul(
                            pp[:, :],
                            wdn_t[:, ic * 2048 + dc * 128:ic * 2048 + dc * 128 + 128],
                            mid_t[:, ic * QT:ic * QT + 512],
                            start=(ic == 0), stop=(ic == 7)),
                            reads=[wdn_b, mid_b], writes=[pp_b])
                    o = dc * QT
                    p.op("dv", lambda q, o=o, pp=pp: q.scalar_tensor_tensor(
                        x_t[:, o:o + 512], pp[:, :], 0.5, x_t[:, o:o + 512],
                        ALU.mult, ALU.add),
                        reads=[pp_b, x_b], writes=[x_b])
            for ch in range(NCH):
                p.op("sy", lambda q, ch=ch, t0=t0: q.dma_start(
                    out=x1_d[ch * 128:(ch + 1) * 128, t0:t0 + QT],
                    in_=x_t[:, ch * QT:(ch + 1) * QT]),
                    reads=[x_b], dma=True)

        p.emit(ctx.enter_context(nc.Block()))
    return nc



def _fm(a):
    return np.ascontiguousarray(a.T)


def _shard_tokens(x):
    out = []
    for core in range(N_CORES):
        b, h = core // 2, core % 2
        out.append(_fm(np.asarray(x[b, h * TOK:(h + 1) * TOK, :])))
    return out


def _bf16(a):
    return np.asarray(a).astype(BF)


def ffn0_inputs(inputs):
    return {
        "wsh_up": _bf16(inputs["L0_ffn_Wsh_up"]),
        "wsh_dn": _bf16(inputs["L0_ffn_Wsh_down"]),
        "wrt_up": _bf16(inputs["L0_ffn_Wrt_up"]),
        "wrt_dn": _bf16(inputs["L0_ffn_Wrt_down"]),
        "wrout": _bf16(inputs["L0_ffn_W_router"]),
    }


def run_ffn0(nc, x_shards, wmap):
    in_maps = [dict(x=np.ascontiguousarray(xs), **wmap) for xs in x_shards]
    res = run_bass_kernel_spmd(nc, in_maps, core_ids=list(range(N_CORES)))
    return [r["x1"] for r in res.results]



def build_ffn1():
    nc = bass.Bass("TRN2", target_bir_lowering=False)
    QT = 512
    NP = TOK // QT
    NIC = 44

    x_d = nc.dram_tensor("x", [D_MODEL, TOK], F32, kind="ExternalInput").ap()
    wg_d = nc.dram_tensor("wg", [D_MODEL, 5632], BF16, kind="ExternalInput").ap()
    wu_d = nc.dram_tensor("wu", [D_MODEL, 5632], BF16, kind="ExternalInput").ap()
    wd_d = nc.dram_tensor("wd", [16, 128, NIC * 128], BF16, kind="ExternalInput").ap()
    gs_d = nc.dram_tensor("gs", [128, NIC], F32, kind="ExternalInput").ap()
    us_d = nc.dram_tensor("us", [128, NIC], F32, kind="ExternalInput").ap()
    ds_d = nc.dram_tensor("ds", [128, 16], F32, kind="ExternalInput").ap()
    x1_d = nc.dram_tensor("x1", [D_MODEL, TOK], F32, kind="ExternalOutput").ap()

    p = Prog(nc)
    ctx = contextlib.ExitStack()
    with ctx:
        c = make_common(ctx, nc, p)
        pools = {}
        pools["row"] = alloc(ctx, nc, p, "ps_row", [1, 512], F32, psum=True)
        pools["bc_m"] = alloc(ctx, nc, p, "ps_bcm", [128, 512], F32, psum=True)
        pools["bc_r"] = alloc(ctx, nc, p, "ps_bcr", [128, 512], F32, psum=True)
        g_ps, g_ps_b = alloc(ctx, nc, p, "ps_g", [128, 512], F32, psum=True)
        u_ps, u_ps_b = alloc(ctx, nc, p, "ps_u", [128, 512], F32, psum=True)
        dn_ps, dn_ps_b = alloc(ctx, nc, p, "ps_dnA", [128, 512], F32, psum=True)
        dn2_ps, dn2_ps_b = alloc(ctx, nc, p, "ps_dnB", [128, 512], F32, psum=True)
        dnp = [(dn_ps, dn_ps_b), (dn2_ps, dn2_ps_b)]

        ln = LNScratch(ctx, nc, p, QT, pools, "ln")
        x_t, x_b = alloc(ctx, nc, p, "xs", [128, NCH * QT], F32)
        xn_t, xn_b = alloc(ctx, nc, p, "xn", [128, NCH * QT], BF16)
        mid_t, mid_b = alloc(ctx, nc, p, "mid", [128, NIC * 512], BF16)
        gsb_t, gsb_b = alloc(ctx, nc, p, "gsb", [128, 512], F32)
        usb_t, usb_b = alloc(ctx, nc, p, "usb", [128, 512], F32)
        wg_t = []
        wg_b = []
        wu_t = []
        wu_b = []
        for i in range(2):
            a, b = alloc(ctx, nc, p, f"wgt{i}", [128, 16 * 512], BF16)
            wg_t.append(a); wg_b.append(b)
            a, b = alloc(ctx, nc, p, f"wut{i}", [128, 16 * 512], BF16)
            wu_t.append(a); wu_b.append(b)
        wd_t = []
        wd_b = []
        for i in range(2):
            a, b = alloc(ctx, nc, p, f"wdt{i}", [128, NIC * 128], BF16)
            wd_t.append(a); wd_b.append(b)
        gs_t, gs_b = alloc(ctx, nc, p, "gst", [128, NIC], F32)
        us_t, us_b = alloc(ctx, nc, p, "ust", [128, NIC], F32)
        ds_t, ds_b = alloc(ctx, nc, p, "dst", [128, 16], F32)

        p.op("sy", lambda q: q.dma_start(out=gs_t[:, :], in_=gs_d[:, :]),
             writes=[gs_b], dma=True)
        p.op("sy", lambda q: q.dma_start(out=us_t[:, :], in_=us_d[:, :]),
             writes=[us_b], dma=True)
        p.op("sy", lambda q: q.dma_start(out=ds_t[:, :], in_=ds_d[:, :]),
             writes=[ds_b], dma=True)
        p.op("dv", lambda q: q.tensor_scalar_mul(ds_t[:, :], ds_t[:, :], 0.5),
             reads=[ds_b], writes=[ds_b])

        NOB = 11

        for pi in range(NP):
            t0 = pi * QT
            for ch in range(NCH):
                p.op("sy", lambda q, ch=ch, t0=t0: q.dma_start(
                    out=x_t[:, ch * QT:(ch + 1) * QT],
                    in_=x_d[ch * 128:(ch + 1) * 128, t0:t0 + QT]),
                    writes=[x_b], dma=True)
            ln.run(p, c, x_t, x_b, xn_t, xn_b, QT)

            for ob in range(NOB):
                pb = ob % 2
                p.op("sy", lambda q, ob=ob, pb=pb: q.dma_start(
                    out=wg_t[pb][:, :].rearrange("p (a m) -> p a m", a=16),
                    in_=wg_d[:, ob * 512:(ob + 1) * 512].rearrange(
                        "(a p) m -> p a m", p=128)),
                    writes=[wg_b[pb]], dma=True)
                p.op("ac", lambda q, ob=ob, pb=pb: q.dma_start(
                    out=wu_t[pb][:, :].rearrange("p (a m) -> p a m", a=16),
                    in_=wu_d[:, ob * 512:(ob + 1) * 512].rearrange(
                        "(a p) m -> p a m", p=128)),
                    writes=[wu_b[pb]], dma=True)
                for icw in range(4):
                    ic = ob * 4 + icw
                    for dc in range(NCH):
                        p.op("pe", lambda q, icw=icw, dc=dc, pb=pb: q.matmul(
                            g_ps[:, :],
                            wg_t[pb][:, dc * 512 + icw * 128:dc * 512 + icw * 128 + 128],
                            xn_t[:, dc * QT:dc * QT + 512],
                            start=(dc == 0), stop=(dc == NCH - 1)),
                            reads=[wg_b[pb], xn_b], writes=[g_ps_b])
                    for dc in range(NCH):
                        p.op("pe", lambda q, icw=icw, dc=dc, pb=pb: q.matmul(
                            u_ps[:, :],
                            wu_t[pb][:, dc * 512 + icw * 128:dc * 512 + icw * 128 + 128],
                            xn_t[:, dc * QT:dc * QT + 512],
                            start=(dc == 0), stop=(dc == NCH - 1)),
                            reads=[wu_b[pb], xn_b], writes=[u_ps_b])
                    p.op("ac", lambda q, ic=ic: q.activation(
                        gsb_t[:, :], g_ps[:, :], AF.Silu, scale=gs_t[:, ic:ic + 1]),
                        reads=[g_ps_b, gs_b], writes=[gsb_b])
                    p.op("dv", lambda q, ic=ic: q.tensor_scalar(
                        usb_t[:, :], u_ps[:, :], us_t[:, ic:ic + 1], None, ALU.mult),
                        reads=[u_ps_b, us_b], writes=[usb_b])
                    p.op("gp", lambda q, ic=ic: q.tensor_mul(
                        mid_t[:, ic * 512:(ic + 1) * 512], gsb_t[:, :], usb_t[:, :]),
                        reads=[gsb_b, usb_b], writes=[mid_b])

            for dc in range(NCH):
                pb = dc % 2
                p.op("sy", lambda q, dc=dc, pb=pb: q.dma_start(
                    out=wd_t[pb][:, :], in_=wd_d[dc]),
                    writes=[wd_b[pb]], dma=True)
                pp, pp_b = dnp[dc % 2]
                for ic in range(NIC):
                    p.op("pe", lambda q, dc=dc, ic=ic, pb=pb, pp=pp: q.matmul(
                        pp[:, :],
                        wd_t[pb][:, ic * 128:ic * 128 + 128],
                        mid_t[:, ic * 512:ic * 512 + 512],
                        start=(ic == 0), stop=(ic == NIC - 1)),
                        reads=[wd_b[pb], mid_b], writes=[pp_b])
                o = dc * QT
                p.op("dv", lambda q, o=o, dc=dc, pp=pp: q.scalar_tensor_tensor(
                    x_t[:, o:o + 512], pp[:, :], ds_t[:, dc:dc + 1],
                    x_t[:, o:o + 512], ALU.mult, ALU.add),
                    reads=[pp[1] if False else dnp[dc % 2][1], ds_b, x_b],
                    writes=[x_b])
            for ch in range(NCH):
                p.op("sy", lambda q, ch=ch, t0=t0: q.dma_start(
                    out=x1_d[ch * 128:(ch + 1) * 128, t0:t0 + QT],
                    in_=x_t[:, ch * QT:(ch + 1) * QT]),
                    reads=[x_b], dma=True)

        p.emit(ctx.enter_context(nc.Block()))
    return nc


def ffn1_inputs(inputs):
    gwT = _bf16(np.ascontiguousarray(np.asarray(inputs["L1_ffn_gate_w"]).T))
    uwT = _bf16(np.ascontiguousarray(np.asarray(inputs["L1_ffn_up_w"]).T))
    dwT = np.asarray(inputs["L1_ffn_down_w"]).T
    dwp = _bf16(np.ascontiguousarray(
        dwT.reshape(44, 128, 16, 128).transpose(2, 1, 0, 3).reshape(16, 128, 44 * 128)))
    return {
        "wg": gwT, "wu": uwT, "wd": dwp,
        "gs": np.ascontiguousarray(
            np.asarray(inputs["L1_ffn_gate_s"], np.float32).reshape(44, 128).T),
        "us": np.ascontiguousarray(
            np.asarray(inputs["L1_ffn_up_s"], np.float32).reshape(44, 128).T),
        "ds": np.ascontiguousarray(
            np.asarray(inputs["L1_ffn_down_s"], np.float32).reshape(16, 128).T),
    }


def run_ffn1(nc, x_shards, wmap):
    in_maps = [dict(x=np.ascontiguousarray(xs), **wmap) for xs in x_shards]
    res = run_bass_kernel_spmd(nc, in_maps, core_ids=list(range(N_CORES)))
    return [r["x1"] for r in res.results]



NH0 = 64
DS0 = 64
DH0 = 64
QC = 128
NC0 = TOK // QC


def _softplus(p, tag, ctx, nc, t, b, P, T, scratch=None):
    if scratch is None:
        az_t, az_b = alloc(ctx, nc, p, f"{tag}_az", [P, T], F32)
    else:
        az_t, az_b = scratch
    p.op("ac", lambda q: q.activation(az_t[:, :], t[:, :], AF.Abs),
         reads=[b], writes=[az_b])
    p.op("ac", lambda q: q.activation(az_t[:, :], az_t[:, :], AF.Exp, scale=-1.0),
         reads=[az_b], writes=[az_b])
    p.op("dv", lambda q: q.tensor_scalar_add(az_t[:, :], az_t[:, :], 1.0),
         reads=[az_b], writes=[az_b])
    p.op("ac", lambda q: q.activation(az_t[:, :], az_t[:, :], AF.Ln),
         reads=[az_b], writes=[az_b])
    p.op("dv", lambda q: q.tensor_scalar_max(t[:, :], t[:, :], 0.0),
         reads=[b], writes=[b])
    p.op("dv", lambda q: q.tensor_add(t[:, :], t[:, :], az_t[:, :]),
         reads=[b, az_b], writes=[b])


def build_att0_proj():
    nc = bass.Bass("TRN2", target_bir_lowering=False)
    QT = 512
    NP = TOK // QT

    x_d = nc.dram_tensor("x", [D_MODEL, TOK], F32, kind="ExternalInput").ap()
    wi_d = nc.dram_tensor("wi", [D_MODEL, 4096], BF16, kind="ExternalInput").ap()
    wp_d = nc.dram_tensor("wp", [D_MODEL, 8256], BF16, kind="ExternalInput").ap()
    xn_d = nc.dram_tensor("xn", [D_MODEL, TOK], BF16, kind="ExternalOutput").ap()
    inp_d = nc.dram_tensor("inp", [TOK, 4096], BF16, kind="ExternalOutput").ap()
    B_d = nc.dram_tensor("Bout", [4096, TOK], BF16, kind="ExternalOutput").ap()
    C_d = nc.dram_tensor("Cout", [4096, TOK], BF16, kind="ExternalOutput").ap()
    dt_d = nc.dram_tensor("dt", [64, TOK], F32, kind="ExternalOutput").ap()

    p = Prog(nc)
    ctx = contextlib.ExitStack()
    with ctx:
        c = make_common(ctx, nc, p)
        pools = {}
        pools["row"] = alloc(ctx, nc, p, "ps_row", [1, 512], F32, psum=True)
        pools["bc_m"] = alloc(ctx, nc, p, "ps_bcm", [128, 512], F32, psum=True)
        pools["bc_r"] = alloc(ctx, nc, p, "ps_bcr", [128, 512], F32, psum=True)
        prj = [alloc(ctx, nc, p, f"ps_p{i}", [128, 512], F32, psum=True)
               for i in range(4)]
        ln = LNScratch(ctx, nc, p, QT, pools, "ln")
        x_t, x_b = alloc(ctx, nc, p, "xs", [128, NCH * QT], F32)
        xn_t, xn_b = alloc(ctx, nc, p, "xnq", [128, NCH * QT], BF16)
        wb_t = []
        wb_b = []
        for i in range(2):
            a, b = alloc(ctx, nc, p, f"wb{i}", [128, 16 * 512], BF16)
            wb_t.append(a)
            wb_b.append(b)
        inpq_t, inpq_b = alloc(ctx, nc, p, "inpq", [128, 4 * 4096], BF16)
        ev_t = []
        ev_b = []
        for i in range(2):
            a, b = alloc(ctx, nc, p, f"ev{i}", [128, 512], BF16)
            ev_t.append(a)
            ev_b.append(b)
        dt_t, dt_b = alloc(ctx, nc, p, "dtq", [64, 512], F32)

        for pi in range(NP):
            t0 = pi * QT
            for ch in range(NCH):
                p.op("sy", lambda q, ch=ch, t0=t0: q.dma_start(
                    out=x_t[:, ch * QT:(ch + 1) * QT],
                    in_=x_d[ch * 128:(ch + 1) * 128, t0:t0 + QT]),
                    writes=[x_b], dma=True)
            ln.run(p, c, x_t, x_b, xn_t, xn_b, QT)
            for ch in range(NCH):
                p.op("gp", lambda q, ch=ch, t0=t0: q.dma_start(
                    out=xn_d[ch * 128:(ch + 1) * 128, t0:t0 + QT],
                    in_=xn_t[:, ch * QT:(ch + 1) * QT]),
                    reads=[xn_b], dma=True)

            for ob in range(8):
                pb = ob % 2
                p.op("sy", lambda q, ob=ob, pb=pb: q.dma_start(
                    out=wb_t[pb][:, :].rearrange("p (a m) -> p a m", a=16),
                    in_=wi_d[:, ob * 512:(ob + 1) * 512].rearrange(
                        "(a p) m -> p a m", p=128)),
                    writes=[wb_b[pb]], dma=True)
                for tt in range(4):
                    pp, pp_b = prj[tt % 2]
                    for dc in range(NCH):
                        p.op("pe", lambda q, ob=ob, tt=tt, dc=dc, pb=pb, pp=pp: q.matmul(
                            pp[:, :],
                            xn_t[:, dc * QT + tt * 128:dc * QT + tt * 128 + 128],
                            wb_t[pb][:, dc * 512:(dc + 1) * 512],
                            start=(dc == 0), stop=(dc == NCH - 1)),
                            reads=[xn_b, wb_b[pb]], writes=[pp_b])
                    p.op("dv", lambda q, ob=ob, tt=tt, pp=pp: q.tensor_copy(
                        inpq_t[:, tt * 4096 + ob * 512:tt * 4096 + ob * 512 + 512],
                        pp[:, :]),
                        reads=[prj[tt % 2][1]], writes=[inpq_b])
            for tt in range(4):
                p.op("sy", lambda q, tt=tt, t0=t0: q.dma_start(
                    out=inp_d[t0 + tt * 128:t0 + tt * 128 + 128, :],
                    in_=inpq_t[:, tt * 4096:(tt + 1) * 4096]),
                    reads=[inpq_b], dma=True)

            for sect, out_d in ((0, B_d), (1, C_d)):
                for ob in range(8):
                    pb = ob % 2
                    col0 = sect * 4096 + ob * 512
                    p.op("sy", lambda q, col0=col0, pb=pb: q.dma_start(
                        out=wb_t[pb][:, :].rearrange("p (a m) -> p a m", a=16),
                        in_=wp_d[:, col0:col0 + 512].rearrange(
                            "(a p) m -> p a m", p=128)),
                        writes=[wb_b[pb]], dma=True)
                    for oc in range(4):
                        pp, pp_b = prj[2 + oc % 2]
                        for dc in range(NCH):
                            p.op("pe", lambda q, oc=oc, dc=dc, pb=pb, pp=pp: q.matmul(
                                pp[:, :],
                                wb_t[pb][:, dc * 512 + oc * 128:dc * 512 + oc * 128 + 128],
                                xn_t[:, dc * QT:dc * QT + 512],
                                start=(dc == 0), stop=(dc == NCH - 1)),
                                reads=[wb_b[pb], xn_b], writes=[pp_b])
                        eb = oc % 2
                        p.op("dv", lambda q, oc=oc, pp=pp, eb=eb: q.tensor_copy(
                            ev_t[eb][:, :], pp[:, :]),
                            reads=[prj[2 + oc % 2][1]], writes=[ev_b[eb]])
                        r0 = ob * 512 + oc * 128
                        p.op("gp", lambda q, r0=r0, t0=t0, out_d=out_d, eb=eb: q.dma_start(
                            out=out_d[r0:r0 + 128, t0:t0 + QT],
                            in_=ev_t[eb][:, :]),
                            reads=[ev_b[eb]], dma=True)
            pb = 0
            p.op("sy", lambda q: q.dma_start(
                out=wb_t[0][:, 0:16 * 64].rearrange("p (a m) -> p a m", a=16),
                in_=wp_d[:, 8192:8256].rearrange("(a p) m -> p a m", p=128)),
                writes=[wb_b[0]], dma=True)
            pp, pp_b = prj[0]
            for dc in range(NCH):
                p.op("pe", lambda q, dc=dc, pp=pp: q.matmul(
                    pp[0:64, :],
                    wb_t[0][:, dc * 64:dc * 64 + 64],
                    xn_t[:, dc * QT:dc * QT + 512],
                    start=(dc == 0), stop=(dc == NCH - 1)),
                    reads=[wb_b[0], xn_b], writes=[pp_b])
            p.op("dv", lambda q, pp=pp: q.tensor_copy(dt_t[:, :], pp[0:64, :]),
                 reads=[prj[0][1]], writes=[dt_b])
            p.op("sy", lambda q, t0=t0: q.dma_start(
                out=dt_d[:, t0:t0 + QT], in_=dt_t[:, :]),
                reads=[dt_b], dma=True)

        p.emit(ctx.enter_context(nc.Block()))
    return nc


def build_att0_states(NH=32, TOKP=4096):
    NCp = TOKP // QC
    nc = bass.Bass("TRN2", target_bir_lowering=False)
    Bt_d = nc.dram_tensor("Bt", [TOKP, NH * DS0], BF16, kind="ExternalInput").ap()
    inp_d = nc.dram_tensor("inp", [TOKP, NH * DH0], BF16, kind="ExternalInput").ap()
    dt_d = nc.dram_tensor("dt", [NH, TOKP], F32, kind="ExternalInput").ap()
    st_d = nc.dram_tensor("st", [NCp, 64, NH * DH0], F32, kind="ExternalOutput").ap()
    sfin_d = nc.dram_tensor("sfin", [64, NH * DH0], F32, kind="ExternalOutput").ap()
    cs_d = nc.dram_tensor("cs", [NH, TOKP], F32, kind="ExternalOutput").ap()
    afm_d = nc.dram_tensor("afm", [NH, NCp], F32, kind="ExternalOutput").ap()
    arow_scr = nc.dram_tensor("arow_scr", [1, NH * NCp], F32, kind="Internal").ap()

    idf = nc.inline_tensor(np.eye(128, dtype=np.float32), "c_idf").ap()
    onef = nc.inline_tensor(np.ones((1, 128), np.float32), "c_onef").ap()
    zrow = nc.inline_tensor(np.zeros((NH, 128), np.float32), "c_zrow").ap()

    p = Prog(nc)
    ctx = contextlib.ExitStack()
    with ctx:
        idf_t, idf_b = alloc(ctx, nc, p, "idf", [128, 128], F32)
        onef_t, onef_b = alloc(ctx, nc, p, "onef", [1, 128], F32)
        zrow_t, zrow_b = alloc(ctx, nc, p, "zrow", [NH, 128], F32)
        p.op("sy", lambda q: q.dma_start(out=idf_t[:, :], in_=idf[:, :]),
             writes=[idf_b], dma=True)
        p.op("sy", lambda q: q.dma_start(out=onef_t[:, :], in_=onef[:, :]),
             writes=[onef_b], dma=True)
        p.op("sy", lambda q: q.dma_start(out=zrow_t[:, :], in_=zrow[:, :]),
             writes=[zrow_b], dma=True)

        dt_t, dt_b = alloc(ctx, nc, p, "dts", [NH, TOKP], F32)
        cs_t, cs_b = alloc(ctx, nc, p, "css", [NH, TOKP], F32)
        dec_t, dec_b = alloc(ctx, nc, p, "dec", [NH, TOKP], F32)
        dtm_ps, dtm_ps_b = alloc(ctx, nc, p, "ps_t", [128, NH], F32, psum=True)
        dectm_t, dectm_b = alloc(ctx, nc, p, "dectm", [128, NCp * NH], BF16)
        afm_t, afm_b = alloc(ctx, nc, p, "afms", [NH, NCp], F32)
        acols_ps, acols_ps_b = alloc(ctx, nc, p, "ps_ac", [64, 512], F32, psum=True)
        acols_t, acols_b = alloc(ctx, nc, p, "acols", [64, NH * NCp], F32)
        arow_t, arow_b = alloc(ctx, nc, p, "arows", [1, NH * NCp], F32)
        bt_t, bt_b = alloc(ctx, nc, p, "bt", [128, NH * DS0], BF16)
        bt2_t, bt2_b = alloc(ctx, nc, p, "bt2", [128, NH * DS0], BF16)
        xt_t, xt_b = alloc(ctx, nc, p, "xt", [128, NH * DH0], BF16)
        xt2_t, xt2_b = alloc(ctx, nc, p, "xt2", [128, NH * DH0], BF16)
        st_ps = [alloc(ctx, nc, p, f"ps_s{i}", [64, 512], F32, psum=True)
                 for i in range(2)]
        stc_t, stc_b = alloc(ctx, nc, p, "stc", [64, NH * DH0], F32)
        S_t, S_b = alloc(ctx, nc, p, "S", [64, NH * DH0], F32)
        sx_t, sx_b = alloc(ctx, nc, p, "sx", [64, NH * DH0], F32)
        arow_scr_b = p.buf("arow_scr")

        p.op("sy", lambda q: q.dma_start(out=dt_t[:, :], in_=dt_d[:, :]),
             writes=[dt_b], dma=True)
        _softplus(p, "sp", ctx, nc, dt_t, dt_b, NH, TOKP)
        p.op("dv", lambda q: q.tensor_scalar_mul(dt_t[:, :], dt_t[:, :], -1.0),
             reads=[dt_b], writes=[dt_b])
        for cc in range(NCp):
            sl = slice(cc * QC, (cc + 1) * QC)
            p.op("dv", lambda q, sl=sl: q.tensor_tensor_scan(
                cs_t[:, sl], dt_t[:, sl], zrow_t[:, :], 0.0, ALU.add, ALU.add),
                reads=[dt_b, zrow_b], writes=[cs_b])
        p.op("sy", lambda q: q.dma_start(out=cs_d[:, :], in_=cs_t[:, :]),
             reads=[cs_b], dma=True)
        p.op("dv", lambda q: q.tensor_copy(
            afm_t[:, :], cs_t[:, :].rearrange("p (c j) -> p c j", j=QC)[:, :, QC - 1]),
            reads=[cs_b], writes=[afm_b])
        p.op("ac", lambda q: q.activation(afm_t[:, :], afm_t[:, :], AF.Exp),
             reads=[afm_b], writes=[afm_b])
        p.op("sy", lambda q: q.dma_start(out=afm_d[:, :], in_=afm_t[:, :]),
             reads=[afm_b], dma=True)
        for cc in range(NCp):
            sl = slice(cc * QC, (cc + 1) * QC)
            p.op("dv", lambda q, sl=sl, cc=cc: q.tensor_scalar(
                dec_t[:, sl], cs_t[:, sl], cs_t[:, cc * QC + QC - 1:cc * QC + QC],
                None, ALU.subtract),
                reads=[cs_b], writes=[dec_b])
        p.op("ac", lambda q: q.activation(dec_t[:, :], dec_t[:, :], AF.Exp, scale=-1.0),
             reads=[dec_b], writes=[dec_b])
        for cc in range(NCp):
            p.op("pe", lambda q, cc=cc: q.transpose(
                dtm_ps[:, :], dec_t[:, cc * QC:(cc + 1) * QC], idf_t[0:NH, 0:NH]),
                reads=[dec_b, idf_b], writes=[dtm_ps_b])
            p.op("dv", lambda q, cc=cc: q.tensor_copy(
                dectm_t[:, cc * NH:(cc + 1) * NH], dtm_ps[:, :]),
                reads=[dtm_ps_b], writes=[dectm_b])
        p.op("sy", lambda q: q.dma_start(
            out=arow_scr.rearrange("a (pp f) -> (a pp) f", pp=64), in_=afm_t[:, :]),
            reads=[afm_b], writes=[arow_scr_b], dma=True)
        p.op("sy", lambda q: q.dma_start(out=arow_t[:, :], in_=arow_scr[:, :]),
             reads=[arow_scr_b], writes=[arow_b], dma=True)
        for i0 in range(0, NH * NCp, 512):
            w = min(512, NH * NCp - i0)
            p.op("pe", lambda q, i0=i0, w=w: q.matmul(
                acols_ps[:, 0:w], onef_t[0:1, 0:64], arow_t[:, i0:i0 + w],
                start=True, stop=True),
                reads=[arow_b, onef_b], writes=[acols_ps_b])
            p.op("dv", lambda q, i0=i0, w=w: q.tensor_copy(
                acols_t[:, i0:i0 + w], acols_ps[:, 0:w]),
                reads=[acols_ps_b], writes=[acols_b])

        p.op("dv", lambda q: q.memset(S_t[:, :], 0.0), writes=[S_b])

        btb = [(bt_t, bt_b), (bt2_t, bt2_b)]
        xtb = [(xt_t, xt_b), (xt2_t, xt2_b)]
        for cc in range(NCp):
            bt, btbuf = btb[cc % 2]
            xt, xtbuf = xtb[cc % 2]
            p.op("sy", lambda q, cc=cc, bt=bt: q.dma_start(
                out=bt[:, :], in_=Bt_d[cc * QC:(cc + 1) * QC, :]),
                writes=[btbuf], dma=True)
            p.op("ac", lambda q, cc=cc, xt=xt: q.dma_start(
                out=xt[:, :], in_=inp_d[cc * QC:(cc + 1) * QC, :]),
                writes=[xtbuf], dma=True)
            p.op("dv", lambda q, cc=cc, bt=bt: q.tensor_tensor(
                bt[:, :].rearrange("p (h s) -> p h s", s=DS0),
                bt[:, :].rearrange("p (h s) -> p h s", s=DS0),
                dectm_t[:, cc * NH:(cc + 1) * NH].unsqueeze(-1).broadcast_to(
                    (QC, NH, DS0)),
                ALU.mult),
                reads=[btbuf, dectm_b], writes=[btbuf])
            for hg in range(NH // 8):
                pp, pp_b = st_ps[hg % 2]
                for hh in range(8):
                    h = hg * 8 + hh
                    p.op("pe", lambda q, h=h, hh=hh, bt=bt, xt=xt, pp=pp: q.matmul(
                        pp[:, hh * 64:(hh + 1) * 64],
                        bt[:, h * 64:(h + 1) * 64],
                        xt[:, h * 64:(h + 1) * 64],
                        start=True, stop=True),
                        reads=[btbuf, xtbuf], writes=[pp_b])
                p.op("dv", lambda q, hg=hg, pp=pp: q.tensor_copy(
                    stc_t[:, hg * 512:(hg + 1) * 512], pp[:, :]),
                    reads=[st_ps[hg % 2][1]], writes=[stc_b])
            p.op("sy", lambda q, cc=cc: q.dma_start(
                out=st_d[cc], in_=stc_t[:, :]),
                reads=[stc_b], dma=True)
            p.op("gp", lambda q, cc=cc: q.tensor_tensor(
                sx_t[:, :].rearrange("p (h s) -> p h s", s=DH0),
                S_t[:, :].rearrange("p (h s) -> p h s", s=DH0),
                acols_t[:, :].rearrange("p (h c) -> p h c", c=NCp)[:, :, cc]
                .unsqueeze(-1).broadcast_to((64, NH, DH0)),
                ALU.mult),
                reads=[S_b, acols_b], writes=[sx_b])
            p.op("gp", lambda q: q.tensor_add(S_t[:, :], sx_t[:, :], stc_t[:, :]),
                 reads=[sx_b, stc_b], writes=[S_b])
        p.op("sy", lambda q: q.dma_start(out=sfin_d[:, :], in_=S_t[:, :]),
             reads=[S_b], dma=True)

        p.emit(ctx.enter_context(nc.Block()))
    return nc


def build_att0_ssd(cut='', NH=32, TOKP=4096):
    Prog.BARRIER_AFTER = frozenset({"pe"})
    nc = bass.Bass("TRN2", target_bir_lowering=False)
    NCp = TOKP // QC
    NPAIR = NH // 2
    X_d = nc.dram_tensor("Xin", [TOKP, NH * DH0], BF16, kind="ExternalInput").ap()
    Bp_d = nc.dram_tensor("Bp", [NCp, 128, NPAIR * 128], BF16, kind="ExternalInput").ap()
    Cp_d = nc.dram_tensor("Cp", [NCp, 128, NPAIR * 128], BF16, kind="ExternalInput").ap()
    csbc_d = nc.dram_tensor("csbc", [NCp, NH // 8, 128, 1024], F32, kind="ExternalInput").ap()
    csrep_d = nc.dram_tensor("csrep", [NCp, NH // 8, 128, 512], F32, kind="ExternalInput").ap()
    clb_d = nc.dram_tensor("clb", [128, NH * NCp], F32, kind="ExternalInput").ap()
    cstm_d = nc.dram_tensor("cstm", [TOKP, NH], F32, kind="ExternalInput").ap()
    st_d = nc.dram_tensor("st", [NCp, 64, NH * DH0], F32, kind="ExternalInput").ap()
    sinit_d = nc.dram_tensor("sinit", [64, NH * DH0], F32, kind="ExternalInput").ap()
    y_d = nc.dram_tensor("y", [TOKP, NH * DH0], BF16, kind="ExternalOutput").ap()

    tri = (np.arange(QC)[:, None] <= np.arange(QC)[None, :]).astype(np.float32)
    tri8 = np.tile(tri[:, None, :], (1, 8, 1)).reshape(128, 8 * QC)
    tri_c = nc.inline_tensor(tri8, "c_tri8").ap()
    negtri8 = (-1e30 * (1.0 - tri8)).astype(np.float32)
    negtri_c = nc.inline_tensor(negtri8, "c_negtri8").ap()

    p = Prog(nc)
    ctx = contextlib.ExitStack()
    with ctx:
        tri_t, tri_b = alloc(ctx, nc, p, "tri", [128, 1024], F32)
        ngt_t, ngt_b = alloc(ctx, nc, p, "ngt", [128, 1024], F32)
        for t, b, s in ((tri_t, tri_b, tri_c), (ngt_t, ngt_b, negtri_c)):
            p.op("sy", lambda q, t=t, s=s: q.dma_start(out=t[:, :], in_=s[:, :]),
                 writes=[b], dma=True)

        sc_ps, sc_ps_b = alloc(ctx, nc, p, "ps_sc", [128, 1024], F32, psum=True)
        y_ps = [alloc(ctx, nc, p, f"ps_y{i}", [128, 512], F32, psum=True)
                for i in range(2)]

        X_t = [alloc(ctx, nc, p, f"X{i}", [128, NH * DH0], BF16) for i in range(2)]
        B_t = [alloc(ctx, nc, p, f"B{i}", [128, NPAIR * 128], BF16) for i in range(2)]
        C_t = [alloc(ctx, nc, p, f"C{i}", [128, NPAIR * 128], BF16) for i in range(2)]
        stc_t = [alloc(ctx, nc, p, f"stc{i}", [128, NH * DH0], F32) for i in range(2)]
        ctm_t = [alloc(ctx, nc, p, f"ctm{i}", [128, NH], F32) for i in range(2)]
        cbc_t = [alloc(ctx, nc, p, f"cbc{i}", [128, 1024], F32) for i in range(2)]
        crp_t = [alloc(ctx, nc, p, f"crp{i}", [128, 512], F32) for i in range(2)]
        rep_t, rep_b = alloc(ctx, nc, p, "repb", [128, 512], BF16)
        ce_t, ce_b = alloc(ctx, nc, p, "ce", [128, NPAIR * 128], BF16)
        csd_t, csd_b = alloc(ctx, nc, p, "csd", [128, 1024], F32)
        e_t, e_b = alloc(ctx, nc, p, "ee", [128, 1024], F32)
        sm_t, sm_b = alloc(ctx, nc, p, "sm", [128, 1024], BF16)
        yc_t, yc_b = alloc(ctx, nc, p, "yc", [128, NH * DH0], BF16)
        S_t, S_b = alloc(ctx, nc, p, "S", [128, NH * DH0], F32)
        sx_t, sx_b = alloc(ctx, nc, p, "sx", [128, NH * DH0], F32)
        sbf_t, sbf_b = alloc(ctx, nc, p, "sbf", [128, NH * DH0], BF16)
        acols_t, acols_b = alloc(ctx, nc, p, "acols", [128, NH * NCp], F32)

        p.op("sy", lambda q: q.dma_start(out=S_t[0:64, :], in_=sinit_d[:, :]),
             writes=[S_b], dma=True)
        p.op("sy", lambda q: q.dma_start(out=S_t[64:128, :], in_=sinit_d[:, :]),
             writes=[S_b], dma=True)
        p.op("gp", lambda q: q.tensor_copy(sbf_t[:, :], S_t[:, :]),
             reads=[S_b], writes=[sbf_b])
        p.op("sy", lambda q: q.dma_start(out=acols_t[:, :], in_=clb_d[:, :]),
             writes=[acols_b], dma=True)
        p.op("ac", lambda q: q.activation(acols_t[:, :], acols_t[:, :], AF.Exp),
             reads=[acols_b], writes=[acols_b])

        for cc in range(NCp):
            db = cc % 2
            xt, xt_b = X_t[db]
            bt, bt_b = B_t[db]
            ct, ct_b = C_t[db]
            st, st_b = stc_t[db]
            ctm, ctm_b = ctm_t[db]
            p.op("sy", lambda q, cc=cc, xt=xt: q.dma_start(
                out=xt[:, :], in_=X_d[cc * QC:(cc + 1) * QC, :]),
                writes=[xt_b], dma=True)
            p.op("sy", lambda q, cc=cc, bt=bt: q.dma_start(
                out=bt[:, :], in_=Bp_d[cc]), writes=[bt_b], dma=True)
            p.op("sy", lambda q, cc=cc, ct=ct: q.dma_start(
                out=ct[:, :], in_=Cp_d[cc]), writes=[ct_b], dma=True)
            p.op("sy", lambda q, cc=cc, st=st: q.dma_start(
                out=st[0:64, :], in_=st_d[cc]), writes=[st_b], dma=True)
            p.op("sy", lambda q, cc=cc, st=st: q.dma_start(
                out=st[64:128, :], in_=st_d[cc]), writes=[st_b], dma=True)
            p.op("sy", lambda q, cc=cc, ctm=ctm: q.dma_start(
                out=ctm[:, :], in_=cstm_d[cc * QC:(cc + 1) * QC, :]),
                writes=[ctm_b], dma=True)
            for hg in range(NH // 8):
                hb = hg % 2
                cbc, cbc_b = cbc_t[hb]
                crp, crp_b = crp_t[hb]
                p.op("sy", lambda q, cc=cc, hg=hg, cbc=cbc: q.dma_start(
                    out=cbc[:, :], in_=csbc_d[cc, hg]), writes=[cbc_b], dma=True)
                p.op("sy", lambda q, cc=cc, hg=hg, crp=crp: q.dma_start(
                    out=crp[:, :], in_=csrep_d[cc, hg]), writes=[crp_b], dma=True)
                if "ce" not in cut:
                    p.op("ac", lambda q, crp=crp: q.activation(
                        rep_t[:, :], crp[:, :], AF.Exp),
                        reads=[crp_b], writes=[rep_b])
                    p.op("dv", lambda q, hg=hg, ct=ct: q.tensor_mul(
                        ce_t[:, hg * 512:(hg + 1) * 512],
                        ct[:, hg * 512:(hg + 1) * 512], rep_t[:, :]),
                        reads=[ct_b, rep_b], writes=[ce_b])
                if "csi" not in cut:
                    p.op("dv", lambda q, hg=hg, ctm=ctm, cbc=cbc: q.tensor_tensor(
                        csd_t[:, :].rearrange("p (h i) -> p h i", i=QC),
                        cbc[:, :].rearrange("p (h i) -> p h i", i=QC),
                        ctm[:, hg * 8:hg * 8 + 8].unsqueeze(-1).broadcast_to(
                            (128, 8, QC)),
                        ALU.subtract),
                        reads=[cbc_b, ctm_b], writes=[csd_b])
                    p.op("gp", lambda q: q.tensor_mul(
                        csd_t[:, :], csd_t[:, :], tri_t[:, :]),
                        reads=[csd_b, tri_b], writes=[csd_b])
                    p.op("gp", lambda q: q.tensor_add(
                        csd_t[:, :], csd_t[:, :], ngt_t[:, :]),
                        reads=[csd_b, ngt_b], writes=[csd_b])
                    p.op("ac", lambda q: q.activation(
                        e_t[:, :], csd_t[:, :], AF.Exp),
                        reads=[csd_b], writes=[e_b])
                for hh in range(8 if 'sc' not in cut else 0):
                    h = hg * 8 + hh
                    pair, half = h // 2, h % 2
                    p.op("pe", lambda q, hh=hh, pair=pair, half=half, bt=bt, ct=ct:
                         q.matmul(
                             sc_ps[:, hh * 128:(hh + 1) * 128],
                             bt[half * 64:half * 64 + 64,
                                pair * 128:(pair + 1) * 128],
                             ct[half * 64:half * 64 + 64,
                                pair * 128:(pair + 1) * 128],
                             start=True, stop=True),
                         reads=[bt_b, ct_b], writes=[sc_ps_b])
                if "sm" not in cut:
                    p.op("dv", lambda q: q.tensor_mul(
                        sm_t[:, :], sc_ps[:, :], e_t[:, :]),
                        reads=[sc_ps_b, e_b], writes=[sm_b])
                if "y" in cut:
                    continue
                pp, pp_b = y_ps[hg % 2]
                for hh in range(8):
                    h = hg * 8 + hh
                    pair, half = h // 2, h % 2
                    p.op("pe", lambda q, hh=hh, h=h, pp=pp, xt=xt: q.matmul(
                        pp[:, hh * 64:(hh + 1) * 64],
                        sm_t[:, hh * 128:(hh + 1) * 128],
                        xt[:, h * 64:(h + 1) * 64],
                        start=True, stop=False),
                        reads=[sm_b, xt_b], writes=[pp_b])
                    p.op("pe", lambda q, hh=hh, h=h, pair=pair, half=half, pp=pp:
                         q.matmul(
                             pp[:, hh * 64:(hh + 1) * 64],
                             ce_t[half * 64:half * 64 + 64,
                                  pair * 128:(pair + 1) * 128],
                             sbf_t[half * 64:half * 64 + 64, h * 64:(h + 1) * 64],
                             start=False, stop=True),
                         reads=[ce_b, sbf_b], writes=[pp_b])
                p.op("dv", lambda q, hg=hg, pp=pp: q.tensor_copy(
                    yc_t[:, hg * 512:(hg + 1) * 512], pp[:, :]),
                    reads=[y_ps[hg % 2][1]], writes=[yc_b])
            p.op("sy", lambda q, cc=cc: q.dma_start(
                out=y_d[cc * QC:(cc + 1) * QC, :], in_=yc_t[:, :]),
                reads=[yc_b], dma=True)
            if "scan" in cut:
                continue
            p.op("gp", lambda q, cc=cc: q.tensor_tensor(
                sx_t[:, :].rearrange("p (h s) -> p h s", s=DH0),
                S_t[:, :].rearrange("p (h s) -> p h s", s=DH0),
                acols_t[:, :].rearrange("p (h c) -> p h c", c=NCp)[:, :, cc]
                .unsqueeze(-1).broadcast_to((128, NH, DH0)),
                ALU.mult),
                reads=[S_b, acols_b], writes=[sx_b])
            p.op("gp", lambda q, st=st: q.tensor_add(S_t[:, :], sx_t[:, :], st[:, :]),
                 reads=[sx_b, st_b], writes=[S_b])
            p.op("gp", lambda q: q.tensor_copy(sbf_t[:, :], S_t[:, :]),
                 reads=[S_b], writes=[sbf_b])

        p.emit(ctx.enter_context(nc.Block()))
    Prog.BARRIER_AFTER = frozenset()
    return nc


def build_att0_gate():
    nc = bass.Bass("TRN2", target_bir_lowering=False)
    QT = 512
    xn_d = nc.dram_tensor("xn", [D_MODEL, TOK], BF16, kind="ExternalInput").ap()
    y_d = nc.dram_tensor("y", [TOK, 4096], BF16, kind="ExternalInput").ap()
    wres_d = nc.dram_tensor("wres", [D_MODEL, 4096], BF16, kind="ExternalInput").ap()
    yg_d = nc.dram_tensor("yg", [TOK, 4096], BF16, kind="ExternalOutput").ap()

    p = Prog(nc)
    ctx = contextlib.ExitStack()
    with ctx:
        prj = [alloc(ctx, nc, p, f"ps_p{i}", [128, 512], F32, psum=True)
               for i in range(2)]
        xn_t, xn_b = alloc(ctx, nc, p, "xnq", [128, NCH * QT], BF16)
        yq_t, yq_b = alloc(ctx, nc, p, "yq", [128, 4 * 4096], BF16)
        sil_t, sil_b = alloc(ctx, nc, p, "sil", [128, 512], BF16)
        wb_t = []
        wb_b = []
        for i in range(2):
            a, b = alloc(ctx, nc, p, f"wb{i}", [128, 16 * 512], BF16)
            wb_t.append(a)
            wb_b.append(b)

        for pi in range(TOK // QT):
            t0 = pi * QT
            for ch in range(NCH):
                p.op("sy", lambda q, ch=ch, t0=t0: q.dma_start(
                    out=xn_t[:, ch * QT:(ch + 1) * QT],
                    in_=xn_d[ch * 128:(ch + 1) * 128, t0:t0 + QT]),
                    writes=[xn_b], dma=True)
            for tt in range(4):
                p.op("ac", lambda q, tt=tt, t0=t0: q.dma_start(
                    out=yq_t[:, tt * 4096:(tt + 1) * 4096],
                    in_=y_d[t0 + tt * 128:t0 + tt * 128 + 128, :]),
                    writes=[yq_b], dma=True)
            for ob in range(8):
                pb = ob % 2
                p.op("sy", lambda q, ob=ob, pb=pb: q.dma_start(
                    out=wb_t[pb][:, :].rearrange("p (a m) -> p a m", a=16),
                    in_=wres_d[:, ob * 512:(ob + 1) * 512].rearrange(
                        "(a p) m -> p a m", p=128)),
                    writes=[wb_b[pb]], dma=True)
                for tt in range(4):
                    pp, pp_b = prj[tt % 2]
                    for dc in range(NCH):
                        p.op("pe", lambda q, tt=tt, dc=dc, pb=pb, pp=pp: q.matmul(
                            pp[:, :],
                            xn_t[:, dc * QT + tt * 128:dc * QT + tt * 128 + 128],
                            wb_t[pb][:, dc * 512:(dc + 1) * 512],
                            start=(dc == 0), stop=(dc == NCH - 1)),
                            reads=[xn_b, wb_b[pb]], writes=[pp_b])
                    p.op("ac", lambda q, pp=pp: q.activation(
                        sil_t[:, :], pp[:, :], AF.Silu),
                        reads=[prj[tt % 2][1]], writes=[sil_b])
                    o = tt * 4096 + ob * 512
                    p.op("gp", lambda q, o=o: q.tensor_mul(
                        yq_t[:, o:o + 512], yq_t[:, o:o + 512], sil_t[:, :]),
                        reads=[yq_b, sil_b], writes=[yq_b])
            for tt in range(4):
                p.op("sy", lambda q, tt=tt, t0=t0: q.dma_start(
                    out=yg_d[t0 + tt * 128:t0 + tt * 128 + 128, :],
                    in_=yq_t[:, tt * 4096:(tt + 1) * 4096]),
                    reads=[yq_b], dma=True)

        p.emit(ctx.enter_context(nc.Block()))
    return nc


def build_outproj(n_inner):
    nc = bass.Bass("TRN2", target_bir_lowering=False)
    QT = 512
    NIC = n_inner // 128
    yg_d = nc.dram_tensor("yg", [n_inner, TOK], BF16, kind="ExternalInput").ap()
    wo_d = nc.dram_tensor("wo", [16, 128, NIC * 128], BF16, kind="ExternalInput").ap()
    xr_d = nc.dram_tensor("xr", [D_MODEL, TOK], F32, kind="ExternalInput").ap()
    xo_d = nc.dram_tensor("xo", [D_MODEL, TOK], F32, kind="ExternalOutput").ap()

    p = Prog(nc)
    ctx = contextlib.ExitStack()
    with ctx:
        prj = [alloc(ctx, nc, p, f"ps_p{i}", [128, 512], F32, psum=True)
               for i in range(2)]
        yg_t, yg_b = alloc(ctx, nc, p, "ygq", [128, NIC * QT], BF16)
        xr_t, xr_b = alloc(ctx, nc, p, "xrq", [128, 512], F32)
        xo_t, xo_b = alloc(ctx, nc, p, "xoq", [128, 512], F32)
        wo_t = []
        wo_b = []
        for i in range(2):
            a, b = alloc(ctx, nc, p, f"wo{i}", [128, NIC * 128], BF16)
            wo_t.append(a)
            wo_b.append(b)

        for pi in range(TOK // QT):
            t0 = pi * QT
            for ic in range(NIC):
                p.op("sy", lambda q, ic=ic, t0=t0: q.dma_start(
                    out=yg_t[:, ic * QT:(ic + 1) * QT],
                    in_=yg_d[ic * 128:(ic + 1) * 128, t0:t0 + QT]),
                    writes=[yg_b], dma=True)
            for dc in range(NCH):
                pb = dc % 2
                p.op("ac", lambda q, dc=dc, pb=pb: q.dma_start(
                    out=wo_t[pb][:, :], in_=wo_d[dc]),
                    writes=[wo_b[pb]], dma=True)
                pp, pp_b = prj[dc % 2]
                for ic in range(NIC):
                    p.op("pe", lambda q, dc=dc, ic=ic, pb=pb, pp=pp: q.matmul(
                        pp[:, :],
                        wo_t[pb][:, ic * 128:(ic + 1) * 128],
                        yg_t[:, ic * QT:(ic + 1) * QT],
                        start=(ic == 0), stop=(ic == NIC - 1)),
                        reads=[wo_b[pb], yg_b], writes=[pp_b])
                p.op("sy", lambda q, dc=dc, t0=t0: q.dma_start(
                    out=xr_t[:, :], in_=xr_d[dc * 128:(dc + 1) * 128, t0:t0 + QT]),
                    writes=[xr_b], dma=True)
                p.op("dv", lambda q, pp=pp, dc=dc: q.tensor_add(
                    xo_t[:, :], pp[:, :], xr_t[:, :]),
                    reads=[prj[dc % 2][1], xr_b], writes=[xo_b])
                p.op("sy", lambda q, dc=dc, t0=t0: q.dma_start(
                    out=xo_d[dc * 128:(dc + 1) * 128, t0:t0 + QT],
                    in_=xo_t[:, :]),
                    reads=[xo_b], dma=True)

        p.emit(ctx.enter_context(nc.Block()))
    return nc



NH1 = 16
DS1 = 16


def build_att1_proj():
    nc = bass.Bass("TRN2", target_bir_lowering=False)
    QT = 512
    x_d = nc.dram_tensor("x", [D_MODEL, TOK], F32, kind="ExternalInput").ap()
    wi_d = nc.dram_tensor("wi", [D_MODEL, 8720], BF16, kind="ExternalInput").ap()
    wn_d = nc.dram_tensor("wn", [128, 16], F32, kind="ExternalInput").ap()
    xaug_d = nc.dram_tensor("xaug", [D_MODEL, TOK], F32, kind="ExternalOutput").ap()
    z_d = nc.dram_tensor("z", [TOK, 4096], BF16, kind="ExternalOutput").ap()
    xi_d = nc.dram_tensor("xi", [4096, TOK], BF16, kind="ExternalOutput").ap()
    dt_d = nc.dram_tensor("dt", [16, TOK], F32, kind="ExternalOutput").ap()
    bc_d = nc.dram_tensor("bc", [512, TOK], BF16, kind="ExternalOutput").ap()

    p = Prog(nc)
    ctx = contextlib.ExitStack()
    with ctx:
        c = make_common(ctx, nc, p)
        pools = {}
        pools["row"] = alloc(ctx, nc, p, "ps_row", [1, 512], F32, psum=True)
        pools["bc_m"] = alloc(ctx, nc, p, "ps_bcm", [128, 512], F32, psum=True)
        pools["bc_r"] = alloc(ctx, nc, p, "ps_bcr", [128, 512], F32, psum=True)
        prj = [alloc(ctx, nc, p, f"ps_p{i}", [128, 512], F32, psum=True)
               for i in range(4)]
        ln = LNScratch(ctx, nc, p, QT, pools, "ln")
        x_t, x_b = alloc(ctx, nc, p, "xs", [128, NCH * QT], F32)
        xn_t, xn_b = alloc(ctx, nc, p, "xnq", [128, NCH * QT], BF16)
        xa_t, xa_b = alloc(ctx, nc, p, "xaq", [128, NCH * QT], F32)
        xr_t, xr_b = alloc(ctx, nc, p, "xrq", [128, NCH * QT], BF16)
        wn_t, wn_b = alloc(ctx, nc, p, "wnt", [128, 16], F32)
        sq2_t, sq2_b = alloc(ctx, nc, p, "sq2", [128, 2 * 512], BF16)
        sq2_b2 = p.buf("sq2b")
        r2row_t, r2row_b = alloc(ctx, nc, p, "r2row", [1, 512], F32)
        tmpr_t, tmpr_b = alloc(ctx, nc, p, "tmpr", [128, 512], F32)
        wb_t = []
        wb_b = []
        for i in range(2):
            a, b = alloc(ctx, nc, p, f"wb{i}", [128, 16 * 512], BF16)
            wb_t.append(a)
            wb_b.append(b)
        zq_t, zq_b = alloc(ctx, nc, p, "zq", [128, 4 * 4096], BF16)
        ev_t = []
        ev_b = []
        for i in range(2):
            a, b = alloc(ctx, nc, p, f"ev{i}", [128, 512], BF16)
            ev_t.append(a)
            ev_b.append(b)
        dt_t, dt_b = alloc(ctx, nc, p, "dtq", [16, 512], F32)

        p.op("sy", lambda q: q.dma_start(out=wn_t[:, :], in_=wn_d[:, :]),
             writes=[wn_b], dma=True)

        sqb = [sq2_b, sq2_b2]
        for pi in range(TOK // QT):
            t0 = pi * QT
            for ch in range(NCH):
                p.op("sy", lambda q, ch=ch, t0=t0: q.dma_start(
                    out=x_t[:, ch * QT:(ch + 1) * QT],
                    in_=x_d[ch * 128:(ch + 1) * 128, t0:t0 + QT]),
                    writes=[x_b], dma=True)
            ln.run(p, c, x_t, x_b, xn_t, xn_b, QT, aug_t=xa_t, aug_b=xa_b)
            for ch in range(NCH):
                p.op("gp", lambda q, ch=ch, t0=t0: q.dma_start(
                    out=xaug_d[ch * 128:(ch + 1) * 128, t0:t0 + QT],
                    in_=xa_t[:, ch * QT:(ch + 1) * QT]),
                    reads=[xa_b], dma=True)
            for ch in range(NCH):
                sl = slice((ch % 2) * 512, (ch % 2) * 512 + 512)
                p.op("ac", lambda q, ch=ch, sl=sl: q.activation(
                    sq2_t[:, sl], xn_t[:, ch * QT:(ch + 1) * QT], AF.Square),
                    reads=[xn_b], writes=[sqb[ch % 2]])
                p.op("pe", lambda q, ch=ch, sl=sl: q.matmul(
                    pools["row"][0][:, :], c["ones"][:, 0:1], sq2_t[:, sl],
                    start=(ch == 0), stop=(ch == NCH - 1)),
                    reads=[sqb[ch % 2]], writes=[pools["row"][1]])
            p.op("dv", lambda q: q.tensor_scalar_mul(
                r2row_t[:, :], pools["row"][0][:, :], 1.0 / D_MODEL),
                reads=[pools["row"][1]], writes=[r2row_b])
            p.op("dv", lambda q: q.tensor_scalar_add(
                r2row_t[:, :], r2row_t[:, :], 1.1920929e-07),
                reads=[r2row_b], writes=[r2row_b])
            p.op("ac", lambda q: q.activation(r2row_t[:, :], r2row_t[:, :], AF.Ln),
                 reads=[r2row_b], writes=[r2row_b])
            p.op("ac", lambda q: q.activation(r2row_t[:, :], r2row_t[:, :], AF.Exp,
                                              scale=-0.5),
                 reads=[r2row_b], writes=[r2row_b])
            p.op("pe", lambda q: q.matmul(
                pools["bc_r"][0][:, :], c["onef"][0:1, :], r2row_t[:, :],
                start=True, stop=True),
                reads=[r2row_b, c["onef_b"]], writes=[pools["bc_r"][1]])
            for ch in range(NCH):
                o = ch * QT
                p.op("dv", lambda q, o=o: q.tensor_mul(
                    tmpr_t[:, :], xn_t[:, o:o + 512], pools["bc_r"][0][:, :]),
                    reads=[xn_b, pools["bc_r"][1]], writes=[tmpr_b])
                p.op("dv", lambda q, o=o, ch=ch: q.tensor_scalar(
                    xr_t[:, o:o + 512], tmpr_t[:, :], wn_t[:, ch:ch + 1], None,
                    ALU.mult),
                    reads=[tmpr_b, wn_b], writes=[xr_b])
            for ob in range(8):
                pb = ob % 2
                p.op("sy", lambda q, ob=ob, pb=pb: q.dma_start(
                    out=wb_t[pb][:, :].rearrange("p (a m) -> p a m", a=16),
                    in_=wi_d[:, ob * 512:(ob + 1) * 512].rearrange(
                        "(a p) m -> p a m", p=128)),
                    writes=[wb_b[pb]], dma=True)
                for tt in range(4):
                    pp, pp_b = prj[tt % 2]
                    for dc in range(NCH):
                        p.op("pe", lambda q, tt=tt, dc=dc, pb=pb, pp=pp: q.matmul(
                            pp[:, :],
                            xr_t[:, dc * QT + tt * 128:dc * QT + tt * 128 + 128],
                            wb_t[pb][:, dc * 512:(dc + 1) * 512],
                            start=(dc == 0), stop=(dc == NCH - 1)),
                            reads=[xr_b, wb_b[pb]], writes=[pp_b])
                    p.op("dv", lambda q, ob=ob, tt=tt, pp=pp: q.tensor_copy(
                        zq_t[:, tt * 4096 + ob * 512:tt * 4096 + ob * 512 + 512],
                        pp[:, :]),
                        reads=[prj[tt % 2][1]], writes=[zq_b])
            for tt in range(4):
                p.op("sy", lambda q, tt=tt, t0=t0: q.dma_start(
                    out=z_d[t0 + tt * 128:t0 + tt * 128 + 128, :],
                    in_=zq_t[:, tt * 4096:(tt + 1) * 4096]),
                    reads=[zq_b], dma=True)
            for ob in range(8):
                pb = ob % 2
                col0 = 4096 + ob * 512
                p.op("sy", lambda q, col0=col0, pb=pb: q.dma_start(
                    out=wb_t[pb][:, :].rearrange("p (a m) -> p a m", a=16),
                    in_=wi_d[:, col0:col0 + 512].rearrange(
                        "(a p) m -> p a m", p=128)),
                    writes=[wb_b[pb]], dma=True)
                for oc in range(4):
                    pp, pp_b = prj[2 + oc % 2]
                    for dc in range(NCH):
                        p.op("pe", lambda q, oc=oc, dc=dc, pb=pb, pp=pp: q.matmul(
                            pp[:, :],
                            wb_t[pb][:, dc * 512 + oc * 128:dc * 512 + oc * 128 + 128],
                            xr_t[:, dc * QT:dc * QT + 512],
                            start=(dc == 0), stop=(dc == NCH - 1)),
                            reads=[wb_b[pb], xr_b], writes=[pp_b])
                    eb = oc % 2
                    p.op("dv", lambda q, pp=pp, eb=eb, oc=oc: q.tensor_copy(
                        ev_t[eb][:, :], pp[:, :]),
                        reads=[prj[2 + oc % 2][1]], writes=[ev_b[eb]])
                    r0 = ob * 512 + oc * 128
                    p.op("gp", lambda q, r0=r0, t0=t0, eb=eb: q.dma_start(
                        out=xi_d[r0:r0 + 128, t0:t0 + QT], in_=ev_t[eb][:, :]),
                        reads=[ev_b[eb]], dma=True)
            p.op("sy", lambda q: q.dma_start(
                out=wb_t[0][:, 0:16 * 512].rearrange("p (a m) -> p a m", a=16),
                in_=wi_d[:, 8208:8720].rearrange("(a p) m -> p a m", p=128)),
                writes=[wb_b[0]], dma=True)
            for oc in range(4):
                pp, pp_b = prj[oc % 2]
                for dc in range(NCH):
                    p.op("pe", lambda q, oc=oc, dc=dc, pp=pp: q.matmul(
                        pp[:, :],
                        wb_t[0][:, dc * 512 + oc * 128:dc * 512 + oc * 128 + 128],
                        xr_t[:, dc * QT:dc * QT + 512],
                        start=(dc == 0), stop=(dc == NCH - 1)),
                        reads=[wb_b[0], xr_b], writes=[pp_b])
                eb = oc % 2
                p.op("dv", lambda q, pp=pp, eb=eb: q.tensor_copy(
                    ev_t[eb][:, :], pp[:, :]),
                    reads=[prj[oc % 2][1]], writes=[ev_b[eb]])
                p.op("gp", lambda q, oc=oc, t0=t0, eb=eb: q.dma_start(
                    out=bc_d[oc * 128:(oc + 1) * 128, t0:t0 + QT],
                    in_=ev_t[eb][:, :]),
                    reads=[ev_b[eb]], dma=True)
            p.op("sy", lambda q: q.dma_start(
                out=wb_t[1][:, 0:16 * 16].rearrange("p (a m) -> p a m", a=16),
                in_=wi_d[:, 8192:8208].rearrange("(a p) m -> p a m", p=128)),
                writes=[wb_b[1]], dma=True)
            pp, pp_b = prj[2]
            for dc in range(NCH):
                p.op("pe", lambda q, dc=dc, pp=pp: q.matmul(
                    pp[0:16, :],
                    wb_t[1][:, dc * 16:dc * 16 + 16],
                    xr_t[:, dc * QT:dc * QT + 512],
                    start=(dc == 0), stop=(dc == NCH - 1)),
                    reads=[wb_b[1], xr_b], writes=[pp_b])
            p.op("dv", lambda q, pp=pp: q.tensor_copy(dt_t[:, :], pp[0:16, :]),
                 reads=[prj[2][1]], writes=[dt_b])
            p.op("sy", lambda q, t0=t0: q.dma_start(
                out=dt_d[:, t0:t0 + QT], in_=dt_t[:, :]),
                reads=[dt_b], dma=True)

        p.emit(ctx.enter_context(nc.Block()))
    return nc


def build_att1_mixa():
    nc = bass.Bass("TRN2", target_bir_lowering=False)
    NCc = TOK // QC
    xi_d = nc.dram_tensor("xi", [4096, TOK], BF16, kind="ExternalInput").ap()
    halo_d = nc.dram_tensor("halo", [4096, 4], BF16, kind="ExternalInput").ap()
    wc_d = nc.dram_tensor("wc", [128, 32 * 4], F32, kind="ExternalInput").ap()
    dt_d = nc.dram_tensor("dt", [16, TOK], F32, kind="ExternalInput").ap()
    dtb_d = nc.dram_tensor("dtb", [16, 1], F32, kind="ExternalInput").ap()
    alog_d = nc.dram_tensor("alog", [16, 1], F32, kind="ExternalInput").ap()
    bhs_d = nc.dram_tensor("bhs", [16, NH1 * TOK], BF16, kind="ExternalInput").ap()
    u_d = nc.dram_tensor("u", [16, NH1 * TOK], BF16, kind="ExternalOutput").ap()
    cs_d = nc.dram_tensor("cs", [16, TOK], F32, kind="ExternalOutput").ap()
    st_d = nc.dram_tensor("st", [16, NH1 * NCc], F32, kind="ExternalOutput").ap()
    sfin_d = nc.dram_tensor("sfin", [16, NH1], F32, kind="ExternalOutput").ap()
    dtx_scr = nc.dram_tensor("dtx_scr", [1, NH1 * TOK], F32, kind="Internal").ap()
    dec_scr = nc.dram_tensor("dec_scr", [1, NH1 * TOK], F32, kind="Internal").ap()
    clast_scr = nc.dram_tensor("clast_scr", [1, NH1 * NCc], F32, kind="Internal").ap()

    hm = np.zeros((128, 16 * 16), BF)
    for cc in range(32):
        hm[:, (cc // 2) * 16 + (cc // 2)] = BF(1.0 / 256.0)
    hm_c = nc.inline_tensor(hm, "c_hm").ap()
    onef_c = nc.inline_tensor(np.ones((1, 128), np.float32), "c_onef1").ap()
    zrow_c = nc.inline_tensor(np.zeros((16, 128), np.float32), "c_zrow1").ap()

    p = Prog(nc)
    ctx = contextlib.ExitStack()
    with ctx:
        hm_t, hm_b = alloc(ctx, nc, p, "hm", [128, 256], BF16)
        onef_t, onef_b = alloc(ctx, nc, p, "onef", [1, 128], F32)
        zrow_t, zrow_b = alloc(ctx, nc, p, "zrow", [16, 128], F32)
        p.op("sy", lambda q: q.dma_start(out=hm_t[:, :], in_=hm_c[:, :]),
             writes=[hm_b], dma=True)
        p.op("sy", lambda q: q.dma_start(out=onef_t[:, :], in_=onef_c[:, :]),
             writes=[onef_b], dma=True)
        p.op("sy", lambda q: q.dma_start(out=zrow_t[:, :], in_=zrow_c[:, :]),
             writes=[zrow_b], dma=True)

        xb_ps, xb_ps_b = alloc(ctx, nc, p, "ps_xb", [16, 4 * 512], F32, psum=True)
        bc_ps, bc_ps_b = alloc(ctx, nc, p, "ps_bc", [16, 512], F32, psum=True)
        xic_t = [alloc(ctx, nc, p, f"xic{i}", [128, TOK + 4], BF16) for i in range(1)]
        acc_t, acc_b = alloc(ctx, nc, p, "acc", [128, TOK], F32)
        xs_t, xs_b = alloc(ctx, nc, p, "xs", [128, TOK], BF16)
        wc_t, wc_b = alloc(ctx, nc, p, "wct", [128, 128], F32)
        xbar_t, xbar_b = alloc(ctx, nc, p, "xbar", [16, TOK], F32)
        dt_t, dt_b = alloc(ctx, nc, p, "dts", [16, TOK], F32)
        dtb_t, dtb_b = alloc(ctx, nc, p, "dtbt", [16, 1], F32)
        al_t, al_b = alloc(ctx, nc, p, "alt", [16, 1], F32)
        cs_t, cs_b = alloc(ctx, nc, p, "css", [16, TOK], F32)
        dec_t, dec_b = alloc(ctx, nc, p, "dec", [16, TOK], F32)
        row_t, row_b = alloc(ctx, nc, p, "rowt", [1, 2048], F32)
        rep_t, rep_b = alloc(ctx, nc, p, "rept", [16, 2048], BF16)
        bhs_t, bhs_b = alloc(ctx, nc, p, "bhst", [16, NH1 * TOK], BF16)
        u_t, u_b = alloc(ctx, nc, p, "ut", [16, NH1 * TOK], BF16)
        udc_t, udc_b = alloc(ctx, nc, p, "udc", [16, 2048], BF16)
        stc_t, stc_b = alloc(ctx, nc, p, "stc1", [16, NH1 * NCc], F32)
        S_t, S_b = alloc(ctx, nc, p, "S1", [16, NH1], F32)
        sx_t, sx_b = alloc(ctx, nc, p, "sx1", [16, NH1], F32)
        acl_t, acl_b = alloc(ctx, nc, p, "acl1", [16, NH1 * NCc], F32)
        dtx_scr_b = p.buf("dtx_scr")
        dec_scr_b = p.buf("dec_scr")
        cl_scr_b = p.buf("cl_scr")

        p.op("sy", lambda q: q.dma_start(out=wc_t[:, :], in_=wc_d[:, :]),
             writes=[wc_b], dma=True)
        p.op("sy", lambda q: q.dma_start(out=dtb_t[:, :], in_=dtb_d[:, :]),
             writes=[dtb_b], dma=True)
        p.op("sy", lambda q: q.dma_start(out=al_t[:, :], in_=alog_d[:, :]),
             writes=[al_b], dma=True)
        p.op("sy", lambda q: q.dma_start(out=bhs_t[:, :], in_=bhs_d[:, :]),
             writes=[bhs_b], dma=True)
        p.op("ac", lambda q: q.activation(al_t[:, :], al_t[:, :], AF.Exp),
             reads=[al_b], writes=[al_b])
        p.op("dv", lambda q: q.tensor_scalar_mul(al_t[:, :], al_t[:, :], -1.0),
             reads=[al_b], writes=[al_b])

        for cc in range(32):
            db = 0
            xic, xic_b = xic_t[db]
            p.op("sy", lambda q, cc=cc, xic=xic: q.dma_start(
                out=xic[:, 4:TOK + 4], in_=xi_d[cc * 128:(cc + 1) * 128, :]),
                writes=[xic_b], dma=True)
            p.op("sy", lambda q, cc=cc, xic=xic: q.dma_start(
                out=xic[:, 0:4], in_=halo_d[cc * 128:(cc + 1) * 128, :]),
                writes=[xic_b], dma=True)
            p.op("dv", lambda q, cc=cc, xic=xic: q.tensor_scalar(
                acc_t[:, :], xic[:, 1:TOK + 1], wc_t[:, cc * 4:cc * 4 + 1], None,
                ALU.mult),
                reads=[xic_b, wc_b], writes=[acc_b])
            for k in (1, 2, 3):
                p.op("dv",
                     lambda q, cc=cc, k=k, xic=xic: q.scalar_tensor_tensor(
                         acc_t[:, :], xic[:, 1 + k:TOK + 1 + k],
                         wc_t[:, cc * 4 + k:cc * 4 + k + 1], acc_t[:, :],
                         ALU.mult, ALU.add),
                     reads=[xic_b, wc_b, acc_b], writes=[acc_b])
            p.op("ac", lambda q: q.activation(xs_t[:, :], acc_t[:, :], AF.Silu),
                 reads=[acc_b], writes=[xs_b])
            for tt in range(TOK // 512):
                p.op("pe", lambda q, cc=cc, tt=tt: q.matmul(
                    xb_ps[:, tt * 512:(tt + 1) * 512],
                    hm_t[:, (cc // 2) * 16:(cc // 2) * 16 + 16],
                    xs_t[:, tt * 512:(tt + 1) * 512],
                    start=(cc == 0), stop=(cc == 31)),
                    reads=[xs_b, hm_b], writes=[xb_ps_b])
                if cc == 31:
                    p.op("dv", lambda q, tt=tt: q.tensor_copy(
                        xbar_t[:, tt * 512:(tt + 1) * 512],
                        xb_ps[:, tt * 512:(tt + 1) * 512]),
                        reads=[xb_ps_b], writes=[xbar_b])

        p.op("sy", lambda q: q.dma_start(out=dt_t[:, :], in_=dt_d[:, :]),
             writes=[dt_b], dma=True)
        p.op("dv", lambda q: q.tensor_scalar(
            dt_t[:, :], dt_t[:, :], dtb_t[:, 0:1], None, ALU.add),
            reads=[dt_b, dtb_b], writes=[dt_b])
        _softplus(p, "sp1", ctx, nc, dt_t, dt_b, 16, TOK,
                  scratch=(acc_t[0:16, :], acc_b))
        p.op("dv", lambda q: q.tensor_scalar(
            cs_t[:, :], dt_t[:, :], al_t[:, 0:1], None, ALU.mult),
            reads=[dt_b, al_b], writes=[cs_b])
        for cc2 in range(NCc):
            sl = slice(cc2 * QC, (cc2 + 1) * QC)
            p.op("dv", lambda q, sl=sl: q.tensor_tensor_scan(
                cs_t[:, sl], cs_t[:, sl], zrow_t[:, :], 0.0, ALU.add, ALU.add),
                reads=[cs_b, zrow_b], writes=[cs_b])
        p.op("sy", lambda q: q.dma_start(out=cs_d[:, :], in_=cs_t[:, :]),
             reads=[cs_b], dma=True)
        for cc2 in range(NCc):
            sl = slice(cc2 * QC, (cc2 + 1) * QC)
            p.op("dv", lambda q, sl=sl, cc2=cc2: q.tensor_scalar(
                dec_t[:, sl], cs_t[:, sl],
                cs_t[:, cc2 * QC + QC - 1:cc2 * QC + QC], None, ALU.subtract),
                reads=[cs_b], writes=[dec_b])
        p.op("ac", lambda q: q.activation(dec_t[:, :], dec_t[:, :], AF.Exp,
                                          scale=-1.0),
             reads=[dec_b], writes=[dec_b])
        dtx_t, dtx_b = dec_t, dec_b
        p.op("dv", lambda q: q.tensor_mul(dtx_t[:, :], dt_t[:, :], xbar_t[:, :]),
             reads=[dt_b, xbar_b], writes=[dtx_b])
        p.op("sy", lambda q: q.dma_start(
            out=dtx_scr.rearrange("a (pp f) -> (a pp) f", pp=16), in_=dtx_t[:, :]),
            reads=[dtx_b], writes=[dtx_scr_b], dma=True)
        p.op("sy", lambda q: q.dma_start(
            out=dec_scr.rearrange("a (pp f) -> (a pp) f", pp=16), in_=dec_t[:, :]),
            reads=[dec_b], writes=[dec_scr_b], dma=True)
        for cc2 in range(NCc):
            p.op("sy", lambda q, cc2=cc2: q.dma_start(
                out=row_t[:, :].rearrange("a (h t) -> a h t", t=QC),
                in_=dtx_scr[0:1, :].rearrange("a (h t) -> a h t", h=NH1)
                [:, :, cc2 * QC:(cc2 + 1) * QC]),
                reads=[dtx_scr_b], writes=[row_b], dma=True)
            for i0 in range(0, 2048, 512):
                p.op("pe", lambda q, i0=i0: q.matmul(
                    bc_ps[:, :], onef_t[0:1, 0:16], row_t[:, i0:i0 + 512],
                    start=True, stop=True),
                    reads=[row_b, onef_b], writes=[bc_ps_b])
                p.op("dv", lambda q, i0=i0: q.tensor_copy(
                    rep_t[:, i0:i0 + 512], bc_ps[:, :]),
                    reads=[bc_ps_b], writes=[rep_b])
            p.op("dv", lambda q, cc2=cc2: q.tensor_tensor(
                u_t[:, :].rearrange("p (h t) -> p h t", h=NH1)
                [:, :, cc2 * QC:(cc2 + 1) * QC],
                bhs_t[:, :].rearrange("p (h t) -> p h t", h=NH1)
                [:, :, cc2 * QC:(cc2 + 1) * QC],
                rep_t[:, :].rearrange("p (h t) -> p h t", t=QC),
                ALU.mult),
                reads=[bhs_b, rep_b], writes=[u_b])
            p.op("sy", lambda q, cc2=cc2: q.dma_start(
                out=row_t[:, :].rearrange("a (h t) -> a h t", t=QC),
                in_=dec_scr[0:1, :].rearrange("a (h t) -> a h t", h=NH1)
                [:, :, cc2 * QC:(cc2 + 1) * QC]),
                reads=[dec_scr_b], writes=[row_b], dma=True)
            for i0 in range(0, 2048, 512):
                p.op("pe", lambda q, i0=i0: q.matmul(
                    bc_ps[:, :], onef_t[0:1, 0:16], row_t[:, i0:i0 + 512],
                    start=True, stop=True),
                    reads=[row_b, onef_b], writes=[bc_ps_b])
                p.op("dv", lambda q, i0=i0: q.tensor_copy(
                    rep_t[:, i0:i0 + 512], bc_ps[:, :]),
                    reads=[bc_ps_b], writes=[rep_b])
            p.op("dv", lambda q, cc2=cc2: q.tensor_tensor(
                udc_t[:, :].rearrange("p (h t) -> p h t", t=QC),
                u_t[:, :].rearrange("p (h t) -> p h t", h=NH1)
                [:, :, cc2 * QC:(cc2 + 1) * QC],
                rep_t[:, :].rearrange("p (h t) -> p h t", t=QC),
                ALU.mult),
                reads=[u_b, rep_b], writes=[udc_b])
            p.op("dv", lambda q, cc2=cc2: q.reduce_sum(
                stc_t[:, :].rearrange("p (h c) -> p h c", c=NCc)[:, :, cc2],
                udc_t[:, :].rearrange("p (h t) -> p h t", t=QC), AX.X),
                reads=[udc_b], writes=[stc_b])
        p.op("sy", lambda q: q.dma_start(out=u_d[:, :], in_=u_t[:, :]),
             reads=[u_b], dma=True)
        p.op("sy", lambda q: q.dma_start(out=st_d[:, :], in_=stc_t[:, :]),
             reads=[stc_b], dma=True)
        p.op("sy", lambda q: q.dma_start(
            out=clast_scr.rearrange("a (pp f) -> (a pp) f", pp=16),
            in_=cs_t[:, :].rearrange("p (cc j) -> p cc j", j=QC)[:, :, QC - 1],
            allow_slow_non_contiguous=True),
            reads=[cs_b], writes=[cl_scr_b], dma=True)
        p.op("sy", lambda q: q.dma_start(
            out=acl_t[0:1, :], in_=clast_scr[:, :]),
            reads=[cl_scr_b], writes=[acl_b], dma=True)
        p.op("pe", lambda q: q.matmul(
            bc_ps[:, 0:NH1 * NCc], onef_t[0:1, 0:16], acl_t[0:1, :],
            start=True, stop=True),
            reads=[acl_b, onef_b], writes=[bc_ps_b])
        p.op("ac", lambda q: q.activation(
            acl_t[:, :], bc_ps[:, 0:NH1 * NCc], AF.Exp),
            reads=[bc_ps_b], writes=[acl_b])
        p.op("dv", lambda q: q.memset(S_t[:, :], 0.0), writes=[S_b])
        for cc2 in range(NCc):
            p.op("gp", lambda q, cc2=cc2: q.tensor_tensor(
                sx_t[:, :], S_t[:, :],
                acl_t[:, :].rearrange("p (h c) -> p h c", c=NCc)[:, :, cc2],
                ALU.mult),
                reads=[S_b, acl_b], writes=[sx_b])
            p.op("gp", lambda q, cc2=cc2: q.tensor_add(
                S_t[:, :], sx_t[:, :],
                stc_t[:, :].rearrange("p (h c) -> p h c", c=NCc)[:, :, cc2]),
                reads=[sx_b, stc_b], writes=[S_b])
        p.op("sy", lambda q: q.dma_start(out=sfin_d[:, :], in_=S_t[:, :]),
             reads=[S_b], dma=True)

        p.emit(ctx.enter_context(nc.Block()))
    return nc


def build_att1_ssd():
    Prog.BARRIER_AFTER = frozenset({"pe"})
    nc = bass.Bass("TRN2", target_bir_lowering=False)
    NCc = TOK // QC
    u_d = nc.dram_tensor("u", [16, NH1 * TOK], BF16, kind="ExternalInput").ap()
    chs_d = nc.dram_tensor("chs", [16, NH1 * TOK], BF16, kind="ExternalInput").ap()
    csbc_d = nc.dram_tensor("csbc", [NCc, 2, 128, 1024], F32, kind="ExternalInput").ap()
    csrep_d = nc.dram_tensor("csrep", [NCc, 16, 2048], F32, kind="ExternalInput").ap()
    clb_d = nc.dram_tensor("clb", [16, NH1 * NCc], F32, kind="ExternalInput").ap()
    cstm_d = nc.dram_tensor("cstm", [TOK, NH1], F32, kind="ExternalInput").ap()
    st_d = nc.dram_tensor("st", [16, NH1 * NCc], F32, kind="ExternalInput").ap()
    sinit_d = nc.dram_tensor("sinit", [16, NH1], F32, kind="ExternalInput").ap()
    y_d = nc.dram_tensor("y", [TOK, 16], F32, kind="ExternalOutput").ap()

    tri = (np.arange(QC)[:, None] <= np.arange(QC)[None, :]).astype(np.float32)
    tri8 = np.tile(tri[:, None, :], (1, 8, 1)).reshape(128, 8 * QC)
    tri_c = nc.inline_tensor(tri8, "c_tri8b").ap()
    negtri8 = (-1e30 * (1.0 - tri8)).astype(np.float32)
    negtri_c = nc.inline_tensor(negtri8, "c_negtri8b").ap()
    onescol_c = nc.inline_tensor(np.ones((128, 1), BF), "c_ones1b").ap()

    p = Prog(nc)
    ctx = contextlib.ExitStack()
    with ctx:
        tri_t, tri_b = alloc(ctx, nc, p, "tri", [128, 1024], F32)
        ngt_t, ngt_b = alloc(ctx, nc, p, "ngt", [128, 1024], F32)
        onesc_t, onesc_b = alloc(ctx, nc, p, "onesc", [128, 1], BF16)
        for t, b, s in ((tri_t, tri_b, tri_c), (ngt_t, ngt_b, negtri_c),
                        (onesc_t, onesc_b, onescol_c)):
            p.op("sy", lambda q, t=t, s=s: q.dma_start(out=t[:, :], in_=s[:, :]),
                 writes=[b], dma=True)

        sc_ps, sc_ps_b = alloc(ctx, nc, p, "ps_sc", [128, 1024], F32, psum=True)
        y_ps, y_ps_b = alloc(ctx, nc, p, "ps_y", [128, 16], F32, psum=True)
        u_t, u_b = alloc(ctx, nc, p, "ut", [16, NH1 * TOK], BF16)
        chs_t, chs_b = alloc(ctx, nc, p, "chst", [16, NH1 * TOK], BF16)
        stc_t, stc_b = alloc(ctx, nc, p, "stc1", [16, NH1 * NCc], F32)
        acl_t, acl_b = alloc(ctx, nc, p, "acl1", [16, NH1 * NCc], F32)
        cbc_t = [alloc(ctx, nc, p, f"cbc{i}", [128, 1024], F32) for i in range(2)]
        crp_t = [alloc(ctx, nc, p, f"crp{i}", [16, 2048], F32) for i in range(2)]
        ctm_t = [alloc(ctx, nc, p, f"ctm{i}", [128, 16], F32) for i in range(2)]
        ce_t, ce_b = alloc(ctx, nc, p, "ce1", [16, 2048], BF16)
        csd_t, csd_b = alloc(ctx, nc, p, "csd", [128, 1024], F32)
        e_t, e_b = alloc(ctx, nc, p, "ee", [128, 1024], F32)
        sm_t, sm_b = alloc(ctx, nc, p, "sm", [128, 1024], BF16)
        yc_t, yc_b = alloc(ctx, nc, p, "yc1", [128, 16], F32)
        S_t, S_b = alloc(ctx, nc, p, "S1", [16, NH1], F32)
        sx_t, sx_b = alloc(ctx, nc, p, "sx1", [16, NH1], F32)
        sbf_t, sbf_b = alloc(ctx, nc, p, "sbf1", [16, NH1], BF16)

        p.op("sy", lambda q: q.dma_start(out=u_t[:, :], in_=u_d[:, :]),
             writes=[u_b], dma=True)
        p.op("sy", lambda q: q.dma_start(out=chs_t[:, :], in_=chs_d[:, :]),
             writes=[chs_b], dma=True)
        p.op("sy", lambda q: q.dma_start(out=stc_t[:, :], in_=st_d[:, :]),
             writes=[stc_b], dma=True)
        p.op("sy", lambda q: q.dma_start(out=acl_t[:, :], in_=clb_d[:, :]),
             writes=[acl_b], dma=True)
        p.op("ac", lambda q: q.activation(acl_t[:, :], acl_t[:, :], AF.Exp),
             reads=[acl_b], writes=[acl_b])
        p.op("sy", lambda q: q.dma_start(out=S_t[:, :], in_=sinit_d[:, :]),
             writes=[S_b], dma=True)
        p.op("gp", lambda q: q.tensor_copy(sbf_t[:, :], S_t[:, :]),
             reads=[S_b], writes=[sbf_b])

        for cc in range(NCc):
            db = cc % 2
            ctm, ctm_b = ctm_t[db]
            crp, crp_b = crp_t[db]
            p.op("sy", lambda q, cc=cc, ctm=ctm: q.dma_start(
                out=ctm[:, :], in_=cstm_d[cc * QC:(cc + 1) * QC, :]),
                writes=[ctm_b], dma=True)
            p.op("sy", lambda q, cc=cc, crp=crp: q.dma_start(
                out=crp[:, :], in_=csrep_d[cc]), writes=[crp_b], dma=True)
            p.op("ac", lambda q, crp=crp: q.activation(
                ce_t[:, :], crp[:, :], AF.Exp),
                reads=[crp_b], writes=[ce_b])
            p.op("dv", lambda q, cc=cc: q.tensor_tensor(
                ce_t[:, :].rearrange("p (h t) -> p h t", t=QC),
                chs_t[:, :].rearrange("p (h t) -> p h t", h=NH1)
                [:, :, cc * QC:(cc + 1) * QC],
                ce_t[:, :].rearrange("p (h t) -> p h t", t=QC),
                ALU.mult),
                reads=[chs_b, ce_b], writes=[ce_b])
            for hg in range(2):
                hb = hg % 2
                cbc, cbc_b = cbc_t[hb]
                p.op("sy", lambda q, cc=cc, hg=hg, cbc=cbc: q.dma_start(
                    out=cbc[:, :], in_=csbc_d[cc, hg]), writes=[cbc_b], dma=True)
                p.op("dv", lambda q, hg=hg, ctm=ctm, cbc=cbc: q.tensor_tensor(
                    csd_t[:, :].rearrange("p (h i) -> p h i", i=QC),
                    cbc[:, :].rearrange("p (h i) -> p h i", i=QC),
                    ctm[:, hg * 8:hg * 8 + 8].unsqueeze(-1).broadcast_to(
                        (128, 8, QC)),
                    ALU.subtract),
                    reads=[cbc_b, ctm_b], writes=[csd_b])
                p.op("gp", lambda q: q.tensor_mul(
                    csd_t[:, :], csd_t[:, :], tri_t[:, :]),
                    reads=[csd_b, tri_b], writes=[csd_b])
                p.op("gp", lambda q: q.tensor_add(
                    csd_t[:, :], csd_t[:, :], ngt_t[:, :]),
                    reads=[csd_b, ngt_b], writes=[csd_b])
                p.op("ac", lambda q: q.activation(
                    e_t[:, :], csd_t[:, :], AF.Exp),
                    reads=[csd_b], writes=[e_b])
                for hh in range(8):
                    h = hg * 8 + hh
                    p.op("pe", lambda q, hh=hh, h=h, cc=cc: q.matmul(
                        sc_ps[:, hh * 128:(hh + 1) * 128],
                        u_t[:, h * TOK + cc * QC:h * TOK + cc * QC + QC],
                        chs_t[:, h * TOK + cc * QC:h * TOK + cc * QC + QC],
                        start=True, stop=True),
                        reads=[u_b, chs_b], writes=[sc_ps_b])
                p.op("dv", lambda q: q.tensor_mul(
                    sm_t[:, :], sc_ps[:, :], e_t[:, :]),
                    reads=[sc_ps_b, e_b], writes=[sm_b])
                for hh in range(8):
                    h = hg * 8 + hh
                    p.op("pe", lambda q, hh=hh, h=h: q.matmul(
                        y_ps[:, h:h + 1],
                        sm_t[:, hh * 128:(hh + 1) * 128],
                        onesc_t[:, 0:1],
                        start=True, stop=False),
                        reads=[sm_b, onesc_b], writes=[y_ps_b])
                    p.op("pe", lambda q, hh=hh, h=h, cc=cc: q.matmul(
                        y_ps[:, h:h + 1],
                        ce_t[:, h * QC:(h + 1) * QC],
                        sbf_t[:, h:h + 1],
                        start=False, stop=True),
                        reads=[ce_b, sbf_b], writes=[y_ps_b])
            p.op("dv", lambda q: q.tensor_copy(yc_t[:, :], y_ps[:, :]),
                 reads=[y_ps_b], writes=[yc_b])
            p.op("sy", lambda q, cc=cc: q.dma_start(
                out=y_d[cc * QC:(cc + 1) * QC, :], in_=yc_t[:, :]),
                reads=[yc_b], dma=True)
            p.op("gp", lambda q, cc=cc: q.tensor_tensor(
                sx_t[:, :], S_t[:, :],
                acl_t[:, :].rearrange("p (h c) -> p h c", c=NCc)[:, :, cc],
                ALU.mult),
                reads=[S_b, acl_b], writes=[sx_b])
            p.op("gp", lambda q, cc=cc: q.tensor_add(
                S_t[:, :], sx_t[:, :],
                stc_t[:, :].rearrange("p (h c) -> p h c", c=NCc)[:, :, cc]),
                reads=[sx_b, stc_b], writes=[S_b])
            p.op("gp", lambda q: q.tensor_copy(sbf_t[:, :], S_t[:, :]),
                 reads=[S_b], writes=[sbf_b])

        p.emit(ctx.enter_context(nc.Block()))
    Prog.BARRIER_AFTER = frozenset()
    return nc


def build_att1_gate():
    nc = bass.Bass("TRN2", target_bir_lowering=False)
    QT = 512
    y_d = nc.dram_tensor("y1", [16, TOK], F32, kind="ExternalInput").ap()
    wh_d = nc.dram_tensor("wh", [16, 4096], BF16, kind="ExternalInput").ap()
    z_d = nc.dram_tensor("z", [TOK, 4096], BF16, kind="ExternalInput").ap()
    yg_d = nc.dram_tensor("yg", [TOK, 4096], BF16, kind="ExternalOutput").ap()

    p = Prog(nc)
    ctx = contextlib.ExitStack()
    with ctx:
        prj = [alloc(ctx, nc, p, f"ps_p{i}", [128, 512], F32, psum=True)
               for i in range(2)]
        y_t, y_b = alloc(ctx, nc, p, "yt", [16, TOK], F32)
        ybf_t, ybf_b = alloc(ctx, nc, p, "ybf", [16, TOK], BF16)
        wh_t, wh_b = alloc(ctx, nc, p, "wht", [16, 4096], BF16)
        zq_t, zq_b = alloc(ctx, nc, p, "zq", [128, 4 * 4096], BF16)
        sil_t, sil_b = alloc(ctx, nc, p, "sil", [128, 512], BF16)
        yi_t, yi_b = alloc(ctx, nc, p, "yi", [128, 512], BF16)

        p.op("sy", lambda q: q.dma_start(out=y_t[:, :], in_=y_d[:, :]),
             writes=[y_b], dma=True)
        p.op("dv", lambda q: q.tensor_copy(ybf_t[:, :], y_t[:, :]),
             reads=[y_b], writes=[ybf_b])
        p.op("sy", lambda q: q.dma_start(out=wh_t[:, :], in_=wh_d[:, :]),
             writes=[wh_b], dma=True)

        for pi in range(TOK // QT):
            t0 = pi * QT
            for tt in range(4):
                p.op("ac", lambda q, tt=tt, t0=t0: q.dma_start(
                    out=zq_t[:, tt * 4096:(tt + 1) * 4096],
                    in_=z_d[t0 + tt * 128:t0 + tt * 128 + 128, :]),
                    writes=[zq_b], dma=True)
            for tt in range(4):
                for ob in range(8):
                    pp, pp_b = prj[ob % 2]
                    p.op("pe", lambda q, tt=tt, ob=ob, t0=t0, pp=pp: q.matmul(
                        pp[:, :],
                        ybf_t[:, t0 + tt * 128:t0 + tt * 128 + 128],
                        wh_t[:, ob * 512:(ob + 1) * 512],
                        start=True, stop=True),
                        reads=[ybf_b, wh_b], writes=[pp_b])
                    p.op("dv", lambda q, pp=pp, ob=ob: q.tensor_copy(
                        yi_t[:, :], pp[:, :]),
                        reads=[prj[ob % 2][1]], writes=[yi_b])
                    o = tt * 4096 + ob * 512
                    p.op("ac", lambda q, o=o: q.activation(
                        sil_t[:, :], zq_t[:, o:o + 512], AF.Silu),
                        reads=[zq_b], writes=[sil_b])
                    p.op("gp", lambda q, o=o: q.tensor_mul(
                        zq_t[:, o:o + 512], yi_t[:, :], sil_t[:, :]),
                        reads=[yi_b, sil_b], writes=[zq_b])
            for tt in range(4):
                p.op("sy", lambda q, tt=tt, t0=t0: q.dma_start(
                    out=yg_d[t0 + tt * 128:t0 + tt * 128 + 128, :],
                    in_=zq_t[:, tt * 4096:(tt + 1) * 4096]),
                    reads=[zq_b], dma=True)

        p.emit(ctx.enter_context(nc.Block()))
    return nc



_NC_CACHE = {}


def _get_nc(name, builder, *args):
    if name not in _NC_CACHE:
        _NC_CACHE[name] = builder(*args)
    return _NC_CACHE[name]


def _run(nc, in_maps):
    return run_bass_kernel_spmd(nc, in_maps, core_ids=list(range(N_CORES))).results


def att0_host(inputs, x1s):
    w_in = np.asarray(inputs["L0_attn_w_in"])
    w_param = np.asarray(inputs["L0_attn_w_param"])
    w_out = np.asarray(inputs["L0_attn_w_out"])

    wiT = _bf16(np.ascontiguousarray(w_in[:4096].T))
    wresT = _bf16(np.ascontiguousarray(w_in[4096:].T))
    idx_B = np.array([h * 129 + 1 + s for h in range(64) for s in range(64)])
    idx_C = np.array([h * 129 + 65 + s for h in range(64) for s in range(64)])
    idx_dt = np.array([h * 129 for h in range(64)])
    perm = np.concatenate([idx_B, idx_C, idx_dt])
    wpT = _bf16(np.ascontiguousarray(w_param[perm].T))
    woT = w_out.T
    wop = _bf16(np.ascontiguousarray(
        woT.reshape(32, 128, 16, 128).transpose(2, 1, 0, 3).reshape(16, 128, 32 * 128)))

    nc1 = _get_nc("att0_proj", build_att0_proj)
    r1 = _run(nc1, [dict(x=np.ascontiguousarray(x1s[i]), wi=wiT, wp=wpT)
                    for i in range(N_CORES)])

    nc2 = _get_nc("att0_states", build_att0_states)
    NHp, TOKP = 32, 4096
    NCp = TOKP // QC
    in2 = []
    host2 = []
    for i in range(N_CORES):
        Bps3 = np.ascontiguousarray(
            r1[i]["Bout"].reshape(64, 64, 32, 64).transpose(2, 3, 0, 1)
            .reshape(32, TOKP, 64))
        Cps3 = np.ascontiguousarray(
            r1[i]["Cout"].reshape(64, 64, 32, 64).transpose(2, 3, 0, 1)
            .reshape(32, TOKP, 64))
        Xps = np.ascontiguousarray(
            r1[i]["inp"].reshape(32, 64, 64, 64).transpose(1, 2, 0, 3)
            .reshape(TOKP, 2048))
        dtp = np.ascontiguousarray(
            r1[i]["dt"].reshape(64, 32, 64).transpose(1, 2, 0).reshape(32, TOKP))
        Bt = np.ascontiguousarray(Bps3.transpose(1, 0, 2).reshape(TOKP, 32 * 64))
        host2.append((Bps3, Cps3, Xps))
        in2.append(dict(Bt=Bt, inp=Xps, dt=dtp))
    r2 = _run(nc2, in2)

    nc3 = _get_nc("att0_ssd", build_att0_ssd)
    in3 = []
    for i in range(N_CORES):
        Bps3, Cps3, Xps = host2[i]
        cs = r2[i]["cs"]
        c3 = cs.reshape(32, NCp, QC).transpose(1, 0, 2)
        csbc = np.ascontiguousarray(np.broadcast_to(
            c3.reshape(NCp, 4, 1, 8 * QC), (NCp, 4, 128, 8 * QC)))
        A2 = c3.reshape(NCp, 4, 4, 2, QC)
        csrep = np.ascontiguousarray(np.broadcast_to(
            A2.transpose(0, 1, 3, 2, 4)[:, :, :, None, :, :],
            (NCp, 4, 2, 64, 4, QC)).reshape(NCp, 4, 128, 512))
        clast = cs.reshape(32, NCp, QC)[:, :, QC - 1]
        clb = np.ascontiguousarray(np.broadcast_to(
            clast.reshape(1, 32 * NCp), (128, 32 * NCp)))
        Bp = np.ascontiguousarray(
            Bps3.reshape(32, NCp, QC, 64).transpose(1, 0, 3, 2)
            .reshape(NCp, 16, 2, 64, QC).transpose(0, 2, 3, 1, 4)
            .reshape(NCp, 128, 16 * QC))
        Cp = np.ascontiguousarray(
            Cps3.reshape(32, NCp, QC, 64).transpose(1, 0, 3, 2)
            .reshape(NCp, 16, 2, 64, QC).transpose(0, 2, 3, 1, 4)
            .reshape(NCp, 128, 16 * QC))
        in3.append(dict(Xin=Xps, Bp=Bp, Cp=Cp, csbc=csbc, csrep=csrep,
                        clb=clb, cstm=np.ascontiguousarray(cs.T),
                        st=r2[i]["st"],
                        sinit=np.zeros((64, 32 * 64), np.float32)))
    r3 = _run(nc3, in3)
    y_tm = [np.ascontiguousarray(
        r3[i]["y"].reshape(64, 64, 32, 64).transpose(2, 0, 1, 3)
        .reshape(2048, 4096)) for i in range(N_CORES)]

    nc4 = _get_nc("att0_gate", build_att0_gate)
    r4 = _run(nc4, [dict(xn=r1[i]["xn"], y=y_tm[i], wres=wresT)
                    for i in range(N_CORES)])

    nc5 = _get_nc("outproj4096", build_outproj, 4096)
    r5 = _run(nc5, [dict(yg=np.ascontiguousarray(r4[i]["yg"].T), wo=wop,
                         xr=np.ascontiguousarray(x1s[i]))
                    for i in range(N_CORES)])
    return [r5[i]["xo"] for i in range(N_CORES)]


def att1_host(inputs, x4s):
    W_in = np.asarray(inputs["L1_attn_W_in"])
    wiT = _bf16(np.ascontiguousarray(W_in.T))
    wnp = np.ascontiguousarray(
        np.asarray(inputs["L1_attn_w_norm"], np.float32).reshape(16, 128).T)
    wcp = np.ascontiguousarray(
        np.asarray(inputs["L1_attn_w_conv"], np.float32).reshape(32, 128, 4)
        .transpose(1, 0, 2).reshape(128, 128))
    dtb = np.ascontiguousarray(
        np.asarray(inputs["L1_attn_dt_bias"], np.float32).reshape(16, 1))
    alog = np.ascontiguousarray(
        np.asarray(inputs["L1_attn_A_log"], np.float32).reshape(16, 1))
    whT = _bf16(np.ascontiguousarray(np.asarray(inputs["L1_attn_W_h2i"]).T))
    woT = np.asarray(inputs["L1_attn_W_out"]).T
    wop = _bf16(np.ascontiguousarray(
        woT.reshape(32, 128, 16, 128).transpose(2, 1, 0, 3)
        .reshape(16, 128, 32 * 128)))

    nc1 = _get_nc("att1_proj", build_att1_proj)
    r1 = _run(nc1, [dict(x=np.ascontiguousarray(x4s[i]), wi=wiT, wn=wnp)
                    for i in range(N_CORES)])

    NCc = TOK // QC
    nc2 = _get_nc("att1_mixa", build_att1_mixa)
    in2 = []
    for i in range(N_CORES):
        halo = np.zeros((4096, 4), BF)
        if i % 2 == 1:
            halo[:, 1:4] = r1[i - 1]["xi"][:, TOK - 3:TOK]
        bhs = np.ascontiguousarray(
            r1[i]["bc"][0:256].reshape(16, 16, TOK).transpose(1, 0, 2)
            .reshape(16, 16 * TOK))
        in2.append(dict(xi=r1[i]["xi"], halo=halo, wc=wcp, dt=r1[i]["dt"],
                        dtb=dtb, alog=alog, bhs=bhs))
    r2 = _run(nc2, in2)

    nc3 = _get_nc("att1_ssd", build_att1_ssd)
    in3 = []
    for i in range(N_CORES):
        cs = r2[i]["cs"]
        chs = np.ascontiguousarray(
            r1[i]["bc"][256:512].reshape(16, 16, TOK).transpose(1, 0, 2)
            .reshape(16, 16 * TOK))
        c3 = cs.reshape(16, NCc, QC).transpose(1, 0, 2)
        csbc = np.ascontiguousarray(np.broadcast_to(
            c3.reshape(NCc, 2, 1, 8 * QC), (NCc, 2, 128, 8 * QC)))
        csrep = np.ascontiguousarray(np.broadcast_to(
            c3.reshape(NCc, 1, 16 * QC), (NCc, 16, 16 * QC)))
        clast = cs.reshape(16, NCc, QC)[:, :, QC - 1]
        clb = np.ascontiguousarray(np.broadcast_to(
            clast.reshape(1, 16 * NCc), (16, 16 * NCc)))
        sinit = (np.zeros((16, 16), np.float32) if i % 2 == 0
                 else r2[i - 1]["sfin"])
        in3.append(dict(u=r2[i]["u"], chs=chs, csbc=csbc, csrep=csrep,
                        clb=clb, cstm=np.ascontiguousarray(cs.T),
                        st=r2[i]["st"], sinit=sinit))
    r3 = _run(nc3, in3)

    nc4 = _get_nc("att1_gate", build_att1_gate)
    r4 = _run(nc4, [dict(y1=np.ascontiguousarray(r3[i]["y"].T), wh=whT,
                         z=r1[i]["z"])
                    for i in range(N_CORES)])

    nc5 = _get_nc("outproj4096", build_outproj, 4096)
    r5 = _run(nc5, [dict(yg=np.ascontiguousarray(r4[i]["yg"].T), wo=wop,
                         xr=r1[i]["xaug"])
                    for i in range(N_CORES)])
    return [r5[i]["xo"] for i in range(N_CORES)]


def kernel(**inputs):
    x = _shard_tokens(np.asarray(inputs["L0_attn_x"], np.float32))
    w0 = ffn0_inputs(inputs)
    w1 = ffn1_inputs(inputs)
    nc_f0 = _get_nc("ffn0", build_ffn0)
    nc_f1 = _get_nc("ffn1", build_ffn1)

    x1 = run_ffn0(nc_f0, x, w0)
    x2 = att0_host(inputs, x1)
    x3 = run_ffn0(nc_f0, x2, w0)
    x4 = run_ffn1(nc_f1, x3, w1)
    x5 = att1_host(inputs, x4)
    x6 = run_ffn1(nc_f1, x5, w1)

    out = np.zeros((BATCH, 4096, D_MODEL), np.float32)
    for core in range(N_CORES):
        b, h = core // 2, core % 2
        out[b, h * TOK:(h + 1) * TOK, :] = x6[core].T
    return out

